# version 01
# baseline (speedup 1.0000x reference)
import numpy as np
import ml_dtypes
import concourse.bass as bass
import concourse.mybir as mybir
from concourse.bass_utils import run_bass_kernel_spmd

F32 = mybir.dt.float32
BF16 = mybir.dt.bfloat16
F16 = mybir.dt.float16
I32 = mybir.dt.int32
U8 = mybir.dt.uint8
AF = mybir.ActivationFunctionType
ALU = mybir.AluOpType
AX = mybir.AxisListType
bfdt = ml_dtypes.bfloat16

NCORES = 8
D = 2048
TOK = 16384
TC = 2048
SEQ = 4096

ENGS = ("sync", "tensor", "vector", "scalar", "gpsimd")


class Sem:
    __slots__ = ("h", "n")

    def __init__(self, nc, name):
        self.h = nc.alloc_semaphore(name=name)
        self.n = 0


class Prog:

    def __init__(self, name):
        self.nc = bass.Bass("TRN2", target_bir_lowering=False)
        self.name = name
        self._uid = 0
        self.phases = [{e: [] for e in ENGS}]
        self.dmasem = {e: Sem(self.nc, f"dma_{e}") for e in ENGS}
        self._phase_dma = [{e: 0 for e in ENGS}]
        self.bar = self.nc.alloc_semaphore(name="bar")

    def uid(self):
        self._uid += 1
        return self._uid

    def dram_in(self, name, shape, dtype):
        return self.nc.dram_tensor(name, list(shape), dtype, kind="ExternalInput").ap()

    def dram_out(self, name, shape, dtype):
        return self.nc.dram_tensor(name, list(shape), dtype, kind="ExternalOutput").ap()

    def dram_tmp(self, name, shape, dtype):
        return self.nc.dram_tensor(f"{name}{self.uid()}", list(shape), dtype,
                                   kind="Internal").ap()

    def sbuf(self, name, shape, dtype):
        return self.nc.alloc_sbuf_tensor(f"{name}_{self.uid()}", list(shape), dtype)

    def psum(self, name, shape, dtype=F32):
        return self.nc.alloc_psum_tensor(f"{name}_{self.uid()}", list(shape), dtype)

    def sem(self, name):
        return Sem(self.nc, f"{name}_{self.uid()}")

    def op(self, eng, fn, inc=None, inc_by=1):
        if inc is not None:
            inc.n += inc_by
            self.phases[-1][eng].append(
                ("comp", lambda e, fn=fn, s=inc.h, b=inc_by: fn(e).then_inc(s, b),
                 (inc.h, inc.n)))
        else:
            self.phases[-1][eng].append(("comp", fn, None))

    def dma(self, eng, fn):
        s = self.dmasem[eng]
        s.n += 16
        self.phases[-1][eng].append(
            ("dma", lambda e, fn=fn, h=s.h: fn(e).then_inc(h, 16), None))

    def wait(self, eng, sem, val=None):
        v = sem.n if val is None else val
        self.phases[-1][eng].append(
            ("ctl", lambda e, h=sem.h, v=v: e.wait_ge(h, v), None))

    def wait_dma(self, eng, src_eng=None):
        s = self.dmasem[src_eng or eng]
        self.phases[-1][eng].append(
            ("ctl", lambda e, h=s.h, v=s.n: e.wait_ge(h, v), None))

    def barrier(self):
        self._phase_dma.append({e: self.dmasem[e].n for e in ENGS})
        self.phases.append({e: [] for e in ENGS})

    def _emit(self):
        self._phase_dma.append({e: self.dmasem[e].n for e in ENGS})
        contrib = []
        for phase in self.phases:
            c = {}
            for e in ENGS:
                has_comp = any(k == "comp" for k, *_ in phase[e])
                has_dma = any(k == "dma" for k, *_ in phase[e])
                c[e] = max((1 if has_comp else 0) + (1 if has_dma else 0), 1)
            contrib.append(c)
        totals = np.cumsum([sum(c.values()) for c in contrib]).tolist()

        nc = self.nc
        with nc.Block() as block:
            for ename in ENGS:
                def body(eng, ename=ename):
                    for i, phase in enumerate(self.phases):
                        if i > 0:
                            eng.wait_ge(self.bar, totals[i - 1])
                        items = phase[ename]
                        last_comp_idx = max(
                            (j for j, (k, *_) in enumerate(items) if k == "comp"),
                            default=None)
                        had_comp = last_comp_idx is not None
                        had_dma = any(k == "dma" for k, *_ in items)
                        tail_wait = None
                        for j, (kind, fn, inc_info) in enumerate(items):
                            inst = fn(eng)
                            if kind == "comp" and j == last_comp_idx:
                                if inc_info is None:
                                    inst.then_inc(self.bar, 1)
                                else:
                                    tail_wait = inc_info
                        if tail_wait is not None:
                            eng.wait_ge(tail_wait[0], tail_wait[1])
                            eng.nop().then_inc(self.bar, 1)
                        if had_dma:
                            eng.wait_ge(self.dmasem[ename].h,
                                        self._phase_dma[i + 1][ename])
                            eng.nop().then_inc(self.bar, 1)
                        if not had_comp and not had_dma:
                            eng.nop().then_inc(self.bar, 1)
                getattr(block, ename)(body)

    def run(self, in_maps, core_ids=None):
        self._emit()
        res = run_bass_kernel_spmd(
            self.nc, in_maps,
            core_ids=list(range(NCORES)) if core_ids is None else core_ids)
        return res.results



def _ln_shared(p, t_cols):
    sh = p.__dict__.setdefault("_ln_shared_pool", {})
    if t_cols not in sh:
        d = {}
        d["s1"] = p.sbuf("lns_s1", [1, t_cols], F32)
        d["s2"] = p.sbuf("lns_s2", [1, t_cols], F32)
        d["ps1"] = p.psum("lns_ps1", [1, 512])
        d["ps2"] = p.psum("lns_ps2", [1, 512])
        d["ones"] = p.sbuf("lns_ones", [128, 1], F32)
        d["xsq"] = p.sbuf("lns_xsq", [128, 512], F32)
        d["A"] = p.sbuf("lns_A", [128, t_cols], F32)
        d["B"] = p.sbuf("lns_B", [128, t_cols], F32)
        d["scr"] = p.dram_tmp("lns_scr", [2, t_cols], F32)
        for nm in ("mean", "ex2", "var", "std", "msq", "ar", "br"):
            d[nm] = p.sbuf("lns_" + nm, [1, t_cols], F32)
        p.op("gpsimd", lambda g: g.memset(d["ones"][:], 1.0))
        sh[t_cols] = d
    return sh[t_cols]


class LNAcc:
    def __init__(self, p, t_cols, tag="ln", want_s1=True):
        self.p = p
        self.T = t_cols
        self.want_s1 = want_s1
        d = _ln_shared(p, t_cols)
        self.d = d
        self.s1 = d["s1"] if want_s1 else None
        self.s2 = d["s2"]
        self.ps1 = d["ps1"] if want_s1 else None
        self.ps2 = d["ps2"]
        self.ones = d["ones"]
        self.xsq = d["xsq"]
        self.A = d["A"]
        self.B = d["B"]
        self.scratch = d["scr"]
        self.mm = p.sem(tag + "_mm")
        self.sq = p.sem(tag + "_sq")
        self.vd = p.sem(tag + "_vd")

    def slice_stats(self, tile_sl, nb, first, last):
        p = self.p
        sl = slice(nb * 512, (nb + 1) * 512)
        p.wait("scalar", self.mm)
        p.op("scalar", lambda s: s.activation(self.xsq[:], tile_sl, AF.Square),
             inc=self.sq)
        p.wait("tensor", self.vd)
        if self.want_s1:
            p.op("tensor", lambda t: t.matmul(
                self.ps1[:], self.ones[:], tile_sl, start=True, stop=True),
                inc=self.mm)
        p.wait("tensor", self.sq)
        p.op("tensor", lambda t: t.matmul(
            self.ps2[:], self.ones[:], self.xsq[:], start=True, stop=True),
            inc=self.mm)
        p.wait("vector", self.mm)
        if first:
            if self.want_s1:
                p.op("vector", lambda v: v.tensor_copy(self.s1[:, sl], self.ps1[:]))
            p.op("vector", lambda v: v.tensor_copy(self.s2[:, sl], self.ps2[:]),
                 inc=self.vd)
        else:
            if self.want_s1:
                p.op("vector", lambda v: v.tensor_tensor(
                    self.s1[:, sl], self.s1[:, sl], self.ps1[:], ALU.add))
            p.op("vector", lambda v: v.tensor_tensor(
                self.s2[:, sl], self.s2[:, sl], self.ps2[:], ALU.add), inc=self.vd)

    def finalize(self, n_feat, eps=1e-5):
        p, T = self.p, self.T
        d = self.d
        mean, ex2, var, std, sq = d["mean"], d["ex2"], d["var"], d["std"], d["msq"]
        arow, brow = d["ar"], d["br"]
        inv = 1.0 / n_feat
        fin = p.sem("lnfin")
        p.op("vector", lambda v: v.tensor_scalar(mean[:], self.s1[:], inv, None, ALU.mult))
        p.op("vector", lambda v: v.tensor_scalar(ex2[:], self.s2[:], inv, None, ALU.mult))
        p.op("vector", lambda v: v.tensor_tensor(sq[:], mean[:], mean[:], ALU.mult))
        p.op("vector", lambda v: v.tensor_tensor(var[:], ex2[:], sq[:], ALU.subtract))
        p.op("vector", lambda v: v.tensor_scalar(var[:], var[:], eps, None, ALU.add),
             inc=fin)
        p.wait("scalar", fin)
        p.op("scalar", lambda s: s.activation(std[:], var[:], AF.Sqrt), inc=fin)
        p.wait("vector", fin)
        p.op("vector", lambda v: v.reciprocal(arow[:], std[:]), inc=fin)
        p.wait("vector", fin)
        p.op("vector", lambda v: v.tensor_tensor(brow[:], mean[:], arow[:], ALU.mult))
        p.op("vector", lambda v: v.tensor_scalar(brow[:], brow[:], -1.0, None, ALU.mult))
        p.barrier()
        p.dma("sync", lambda e: e.dma_start(out=self.scratch[0:1, :], in_=arow[:1, :]))
        p.dma("sync", lambda e: e.dma_start(out=self.scratch[1:2, :], in_=brow[:1, :]))
        p.barrier()
        p.dma("sync", lambda e: e.dma_start(
            out=self.A[:], in_=self.scratch[0:1, :].broadcast_to([128, T])))
        p.dma("sync", lambda e: e.dma_start(
            out=self.B[:], in_=self.scratch[1:2, :].broadcast_to([128, T])))
        p.barrier()

    def rms_finalize(self, n_feat, eps):
        p, T = self.p, self.T
        ms = self.d["mean"]
        rt = self.d["std"]
        fin = p.sem("rmsfin")
        p.op("vector", lambda v: v.tensor_scalar(ms[:], self.s2[:], 1.0 / n_feat,
                                                 eps, ALU.mult, ALU.add), inc=fin)
        p.wait("scalar", fin)
        p.op("scalar", lambda s: s.activation(rt[:], ms[:], AF.Sqrt), inc=fin)
        p.wait("vector", fin)
        p.op("vector", lambda v: v.reciprocal(ms[:], rt[:]), inc=fin)
        p.wait("vector", fin)
        p.barrier()
        p.dma("sync", lambda e: e.dma_start(out=self.scratch[0:1, :], in_=ms[:1, :]))
        p.barrier()
        p.dma("sync", lambda e: e.dma_start(
            out=self.A[:], in_=self.scratch[0:1, :].broadcast_to([128, T])))
        p.barrier()



def emit_proj_stream(p, h_dram, w_dram, K_in, M_total, T, epilogue, tag="pj",
                     nb_list=None, extra_phase_work=None):
    KC = K_in // 128
    NT = T // 512
    MT = M_total // 128
    if nb_list is None:
        nb_list = list(range(NT))
    hb = p.sbuf(f"{tag}_h", [128, KC, 512], BF16)
    wb = [p.sbuf(f"{tag}_w{i}", [128, KC, 128], BF16) for i in range(2)]
    ps = [p.psum(f"{tag}_ps{i}", [128, 512]) for i in range(2)]
    mmd = p.sem(tag + "_mmd")
    h4 = h_dram.rearrange("(kc pp) t -> pp kc t", pp=128)
    w4 = w_dram.rearrange("(kc pp) m -> pp kc m", pp=128)
    for ni, nb in enumerate(nb_list):
        p.dma("sync", lambda e, nb=nb: e.dma_start(
            out=hb[:], in_=h4[:, :, nb * 512:(nb + 1) * 512]))
        p.dma("gpsimd", lambda e: e.dma_start(out=wb[0][:], in_=w4[:, :, 0:128]))
        p.barrier()
        for mi in range(MT):
            par = mi % 2
            if mi + 1 < MT:
                p.dma("gpsimd", lambda e, mi=mi, b=wb[1 - par]: e.dma_start(
                    out=b[:], in_=w4[:, :, (mi + 1) * 128:(mi + 2) * 128]))
            for kc in range(KC):
                p.op("tensor",
                     lambda t, b=wb[par], kc=kc, s=(kc == 0), st=(kc == KC - 1),
                     pp=ps[par]: t.matmul(pp[:], b[:, kc, :], hb[:, kc, :],
                                          start=s, stop=st),
                     inc=(mmd if kc == KC - 1 else None))
            p.wait("vector", mmd)
            p.wait("scalar", mmd)
            epilogue(mi, nb, ps[par])
            p.barrier()
        if extra_phase_work is not None:
            extra_phase_work(nb)



def launch_ln(x_fm):
    p = Prog("ln0")
    x_in = p.dram_in("x", [D, TC], F32)
    xn_out = p.dram_out("xn", [D, TC], BF16)
    acc = LNAcc(p, TC)
    xa = [p.sbuf(f"xa{i}", [128, TC], F32) for i in range(2)]
    xn_t = [p.sbuf(f"xnt{i}", [128, TC], BF16) for i in range(2)]
    tmp = p.sbuf("tmp", [128, TC], F32)
    p.dma("sync", lambda e: e.dma_start(out=xa[0][:], in_=x_in[0:128, :]))
    p.barrier()
    for ci in range(16):
        par = ci % 2
        if ci + 1 < 16:
            p.dma("sync", lambda e, ci=ci, b=xa[1 - par]: e.dma_start(
                out=b[:], in_=x_in[(ci + 1) * 128:(ci + 2) * 128, :]))
        for nb in range(4):
            acc.slice_stats(xa[par][:, nb * 512:(nb + 1) * 512], nb,
                            first=(ci == 0), last=(ci == 15))
        p.barrier()
    acc.finalize(D)
    p.dma("sync", lambda e: e.dma_start(out=xa[0][:], in_=x_in[0:128, :]))
    p.barrier()
    st = p.sem("st")
    for ci in range(16):
        par = ci % 2
        if ci + 1 < 16:
            p.dma("sync", lambda e, ci=ci, b=xa[1 - par]: e.dma_start(
                out=b[:], in_=x_in[(ci + 1) * 128:(ci + 2) * 128, :]))
        acc.apply = None
        p.op("vector", lambda v, par=par: v.tensor_tensor(
            tmp[:], xa[par][:], acc.A[:], ALU.mult))
        p.op("vector", lambda v, par=par: v.tensor_tensor(
            xn_t[par][:], tmp[:], acc.B[:], ALU.add), inc=st)
        p.wait("scalar", st)
        p.dma("scalar", lambda e, ci=ci, b=xn_t[par]: e.dma_start(
            out=xn_out[ci * 128:(ci + 1) * 128, :], in_=b[:]))
        p.barrier()
    in_maps = [{"x": np.ascontiguousarray(x_fm[:, c * TC:(c + 1) * TC])}
               for c in range(NCORES)]
    res = p.run(in_maps)
    return np.concatenate([r["xn"] for r in res], axis=1)


def _emit_res_ln_write(p, y_dram, x_in, xp_out, xn_out, pre_scale_A=None):
    acc = LNAcc(p, TC, tag="lnT")
    xc = p.sbuf("xc", [128, TC], F32)
    yc = p.sbuf("yc", [128, TC], F32)
    xp_t = [p.sbuf(f"xpt{i}", [128, TC], F32) for i in range(2)]
    xn_t = [p.sbuf(f"xnt{i}", [128, TC], BF16) for i in range(2)]
    tmp = p.sbuf("tmpT", [128, TC], F32)
    ev = p.sem("resln")
    p.barrier()
    for mi in range(16):
        par = mi % 2
        p.dma("sync", lambda e, mi=mi: e.dma_start(
            out=xc[:], in_=x_in[mi * 128:(mi + 1) * 128, :]))
        p.dma("scalar", lambda e, mi=mi: e.dma_start(
            out=yc[:], in_=y_dram[mi * 128:(mi + 1) * 128, :]))
        p.wait_dma("vector", "sync")
        p.wait_dma("vector", "scalar")
        if pre_scale_A is not None:
            p.op("vector", lambda v: v.tensor_tensor(tmp[:], yc[:], pre_scale_A[:],
                                                     ALU.mult))
            p.op("vector", lambda v, par=par: v.tensor_tensor(
                xp_t[par][:], tmp[:], xc[:], ALU.add), inc=ev)
        else:
            p.op("vector", lambda v, par=par: v.tensor_tensor(
                xp_t[par][:], yc[:], xc[:], ALU.add), inc=ev)
        p.wait("scalar", ev)
        p.wait("tensor", ev)
        for nb in range(4):
            acc.slice_stats(xp_t[par][:, nb * 512:(nb + 1) * 512], nb,
                            first=(mi == 0), last=(mi == 15))
        p.wait("sync", ev)
        p.dma("sync", lambda e, mi=mi, b=xp_t[par]: e.dma_start(
            out=xp_out[mi * 128:(mi + 1) * 128, :], in_=b[:]))
        p.barrier()
    if xn_out is None:
        return
    acc.finalize(D)
    st = p.sem("stT")
    for ci in range(16):
        par = ci % 2
        p.dma("sync", lambda e, ci=ci, b=xp_t[par]: e.dma_start(
            out=b[:], in_=xp_out[ci * 128:(ci + 1) * 128, :]))
        p.barrier()
        p.op("vector", lambda v, par=par: v.tensor_tensor(
            tmp[:], xp_t[par][:], acc.A[:], ALU.mult))
        p.op("vector", lambda v, par=par: v.tensor_tensor(
            xn_t[par][:], tmp[:], acc.B[:], ALU.add), inc=st)
        p.wait("scalar", st)
        p.dma("scalar", lambda e, ci=ci, b=xn_t[par]: e.dma_start(
            out=xn_out[ci * 128:(ci + 1) * 128, :], in_=b[:]))
        p.barrier()


def launch_proj_res_ln(h_fm, w_T, bias, x_fm, rms_pre_eps=None, skip_ln=False):
    K_in = h_fm.shape[0]
    p = Prog("prl")
    wdt = F32 if w_T.dtype == np.float32 else BF16
    h_in = p.dram_in("h", [K_in, TC], BF16)
    w_in = p.dram_in("w", [K_in, D], wdt)
    x_in = p.dram_in("x", [D, TC], F32)
    xp_out = p.dram_out("xp", [D, TC], F32)
    xn_out = None if skip_ln else p.dram_out("xn", [D, TC], BF16)
    y_dram = p.dram_tmp("y", [D, TC], F32)
    bt = None
    if bias is not None:
        b_in = p.dram_in("b", [128, 16], F32)
        bt = p.sbuf("bt", [128, 16], F32)
        p.dma("sync", lambda e: e.dma_start(out=bt[:], in_=b_in))
    yslice = p.sbuf("ysl", [128, 512], F32)
    racc = LNAcc(p, TC, tag="rms", want_s1=False) if rms_pre_eps is not None else None
    ev = p.sem("ev")
    p.barrier()

    def epilogue(mi, nb, ps):
        if bt is not None:
            p.op("vector", lambda v: v.tensor_scalar(
                yslice[:], ps[:], bt[:, mi:mi + 1], None, ALU.add), inc=ev)
        else:
            p.op("vector", lambda v: v.tensor_copy(yslice[:], ps[:]), inc=ev)
        if racc is not None:
            p.wait("scalar", ev)
            p.wait("tensor", ev)
            racc.slice_stats(yslice[:, :], nb, first=(mi == 0), last=(mi == 15))
        p.wait("sync", ev)
        p.dma("sync", lambda e, mi=mi, nb=nb: e.dma_start(
            out=y_dram[mi * 128:(mi + 1) * 128, nb * 512:(nb + 1) * 512],
            in_=yslice[:]))

    emit_proj_stream(p, h_in, w_in, K_in, D, TC, epilogue, tag="prj")
    p.barrier()
    A = None
    if racc is not None:
        racc.rms_finalize(D, rms_pre_eps)
        A = racc.A
    _emit_res_ln_write(p, y_dram, x_in, xp_out, xn_out, pre_scale_A=A)

    in_maps = []
    for c in range(NCORES):
        m = {"h": np.ascontiguousarray(h_fm[:, c * TC:(c + 1) * TC]),
             "w": w_T,
             "x": np.ascontiguousarray(x_fm[:, c * TC:(c + 1) * TC])}
        if bias is not None:
            m["b"] = np.ascontiguousarray(
                bias.astype(np.float32).reshape(16, 128).T)
        in_maps.append(m)
    res = p.run(in_maps)
    xp = np.concatenate([r["xp"] for r in res], axis=1)
    xn = (None if skip_ln else np.concatenate([r["xn"] for r in res], axis=1))
    return xp, xn


def launch_ffn_up(xn_fm, wg_T, wu_T):
    I = wg_T.shape[1]
    assert I % 128 == 0
    p = Prog("ffnup")
    xn_in = p.dram_in("xn", [D, TC], BF16)
    wg_in = p.dram_in("wg", [D, I], BF16)
    wu_in = p.dram_in("wu", [D, I], BF16)
    h_out = p.dram_out("h", [I, TC], BF16)

    KC = D // 128
    MT = I // 128
    hb = p.sbuf("hb", [128, KC, 512], BF16)
    wb = [p.sbuf(f"wb{i}", [128, KC, 128], BF16) for i in range(4)]
    psg = [p.psum(f"psg{i}", [128, 512]) for i in range(2)]
    psu = [p.psum(f"psu{i}", [128, 512]) for i in range(2)]
    gs = p.sbuf("gs", [128, 512], F32)
    us = p.sbuf("us", [128, 512], F32)
    hs = p.sbuf("hs", [128, 512], BF16)
    mmd = p.sem("mmd")
    ev = p.sem("ev")
    xn4 = xn_in.rearrange("(kc pp) t -> pp kc t", pp=128)
    wg4 = wg_in.rearrange("(kc pp) m -> pp kc m", pp=128)
    wu4 = wu_in.rearrange("(kc pp) m -> pp kc m", pp=128)
    for nb in range(4):
        p.dma("sync", lambda e, nb=nb: e.dma_start(
            out=hb[:], in_=xn4[:, :, nb * 512:(nb + 1) * 512]))
        p.dma("gpsimd", lambda e: e.dma_start(out=wb[0][:], in_=wg4[:, :, 0:128]))
        p.dma("gpsimd", lambda e: e.dma_start(out=wb[2][:], in_=wu4[:, :, 0:128]))
        p.barrier()
        for mi in range(MT):
            par = mi % 2
            if mi + 1 < MT:
                sl = slice((mi + 1) * 128, (mi + 2) * 128)
                p.dma("gpsimd", lambda e, sl=sl, b=wb[1 - par]: e.dma_start(
                    out=b[:], in_=wg4[:, :, sl]))
                p.dma("gpsimd", lambda e, sl=sl, b=wb[3 - par]: e.dma_start(
                    out=b[:], in_=wu4[:, :, sl]))
            for kc in range(KC):
                p.op("tensor", lambda t, b=wb[par], kc=kc, s=(kc == 0),
                     st=(kc == KC - 1), pp=psg[par]: t.matmul(
                         pp[:], b[:, kc, :], hb[:, kc, :], start=s, stop=st),
                     inc=(mmd if kc == KC - 1 else None))
            for kc in range(KC):
                p.op("tensor", lambda t, b=wb[2 + par], kc=kc, s=(kc == 0),
                     st=(kc == KC - 1), pp=psu[par]: t.matmul(
                         pp[:], b[:, kc, :], hb[:, kc, :], start=s, stop=st),
                     inc=(mmd if kc == KC - 1 else None))
            p.wait("scalar", mmd)
            p.op("scalar", lambda s, pp=psg[par]: s.activation(gs[:], pp[:], AF.Silu))
            p.op("scalar", lambda s, pp=psu[par]: s.activation(us[:], pp[:], AF.Identity),
                 inc=ev)
            p.wait("vector", ev)
            p.op("vector", lambda v: v.tensor_tensor(hs[:], gs[:], us[:], ALU.mult),
                 inc=ev)
            p.wait("sync", ev)
            p.dma("sync", lambda e, mi=mi, nb=nb: e.dma_start(
                out=h_out[mi * 128:(mi + 1) * 128, nb * 512:(nb + 1) * 512],
                in_=hs[:]))
            p.barrier()
    in_maps = [{"xn": np.ascontiguousarray(xn_fm[:, c * TC:(c + 1) * TC]),
                "wg": wg_T, "wu": wu_T} for c in range(NCORES)]
    res = p.run(in_maps)
    return np.concatenate([r["h"] for r in res], axis=1)


def launch_ghost_dequant(gate_idxT, up_idxT, down_idxT, lut):
    p = Prog("gdq")
    RG = D // NCORES
    RD = 5504 // NCORES
    g_in = p.dram_in("g", [RG, 5504], I32)
    u_in = p.dram_in("u", [RG, 5504], I32)
    d_in = p.dram_in("d", [RD, D], I32)
    lut_in = p.dram_in("lut", [1, 16], F32)
    g_out = p.dram_out("go", [RG, 5504], BF16)
    u_out = p.dram_out("uo", [RG, 5504], BF16)
    d_out = p.dram_out("do", [RD, D], BF16)
    lutb = p.sbuf("lutb", [128, 16], F32)
    p.dma("gpsimd", lambda e: e.dma_start(out=lutb[:], in_=lut_in.broadcast_to([128, 16])))
    p.barrier()

    idxt = p.sbuf("idxt", [128, 5504], I32)
    idxf = p.sbuf("idxf", [128, 5504], F32)
    eqm = p.sbuf("eqm", [128, 5504], F32)
    deq = p.sbuf("deq", [128, 5504], F32)
    deqb = p.sbuf("deqb", [128, 5504], BF16)

    def dq_rows(src, dst, nrows, ncols):
        nt = (nrows + 127) // 128
        for i in range(nt):
            r0 = i * 128
            r1 = min(nrows, r0 + 128)
            pr = r1 - r0
            cw = p.sem("cw")
            p.dma("sync", lambda e, r0=r0, pr=pr: e.dma_start(
                out=idxt[:pr, :ncols], in_=src[r0:r0 + pr, :]))
            p.barrier()
            p.op("vector", lambda v, pr=pr: v.tensor_copy(
                idxf[:pr, :ncols], idxt[:pr, :ncols]))
            p.op("vector", lambda v, pr=pr: v.tensor_scalar(
                deq[:pr, :ncols], idxf[:pr, :ncols], 0.0, lutb[:pr, 0:1],
                ALU.is_equal, ALU.mult))
            for j in range(1, 16):
                p.op("vector", lambda v, j=j, pr=pr: v.tensor_scalar(
                    eqm[:pr, :ncols], idxf[:pr, :ncols], float(j), lutb[:pr, j:j + 1],
                    ALU.is_equal, ALU.mult))
                p.op("vector", lambda v, pr=pr: v.tensor_tensor(
                    deq[:pr, :ncols], deq[:pr, :ncols], eqm[:pr, :ncols], ALU.add))
            p.op("vector", lambda v, pr=pr: v.tensor_copy(
                deqb[:pr, :ncols], deq[:pr, :ncols]), inc=cw)
            p.wait("sync", cw)
            p.dma("sync", lambda e, r0=r0, pr=pr: e.dma_start(
                out=dst[r0:r0 + pr, :], in_=deqb[:pr, :ncols]))
            p.barrier()

    dq_rows(g_in, g_out, RG, 5504)
    dq_rows(u_in, u_out, RG, 5504)
    dq_rows(d_in, d_out, RD, D)

    in_maps = []
    for c in range(NCORES):
        in_maps.append({
            "g": np.ascontiguousarray(gate_idxT[c * RG:(c + 1) * RG, :]),
            "u": np.ascontiguousarray(up_idxT[c * RG:(c + 1) * RG, :]),
            "d": np.ascontiguousarray(down_idxT[c * RD:(c + 1) * RD, :]),
            "lut": lut.reshape(1, 16).astype(np.float32)})
    res = p.run(in_maps)
    wg = np.concatenate([r["go"] for r in res], axis=0)
    wu = np.concatenate([r["uo"] for r in res], axis=0)
    wd = np.concatenate([r["do"] for r in res], axis=0)
    return wg, wu, wd



def launch_qkv(xn_fm, wq_T, wk_T, wv_T):
    p = Prog("qkv")
    xn_in = p.dram_in("xn", [D, TC], BF16)
    wq_in = p.dram_in("wq", [D, D], F32)
    wk_in = p.dram_in("wk", [D, D], F32)
    wv_in = p.dram_in("wv", [D, D], F32)
    q_out = p.dram_out("q", [D, TC], BF16)
    k_out = p.dram_out("k", [D, TC], BF16)
    v_out = p.dram_out("v", [TC, D], BF16)

    qs = p.sbuf("qs", [128, 512], BF16)
    ev = p.sem("ev")

    def mk_epilogue(out_dram):
        def epi(mi, nb, ps):
            p.op("vector", lambda v: v.tensor_copy(qs[:], ps[:]), inc=ev)
            p.wait("sync", ev)
            p.dma("sync", lambda e, mi=mi, nb=nb: e.dma_start(
                out=out_dram[mi * 128:(mi + 1) * 128, nb * 512:(nb + 1) * 512],
                in_=qs[:]))
        return epi

    emit_proj_stream(p, xn_in, wq_in, D, D, TC, mk_epilogue(q_out), tag="q")
    p.barrier()
    emit_proj_stream(p, xn_in, wk_in, D, D, TC, mk_epilogue(k_out), tag="k")
    p.barrier()
    KC = D // 128
    xb = p.sbuf("xb", [128, KC, 128], BF16)
    wvb = p.sbuf("wvb", [128, KC, D], BF16)
    psv = [p.psum(f"psv{i}", [128, 512]) for i in range(2)]
    vs = p.sbuf("vs", [128, 512], BF16)
    mmv = p.sem("mmv")
    evv = p.sem("evv")
    xn4 = xn_in.rearrange("(kc pp) t -> pp kc t", pp=128)
    wv4 = wv_in.rearrange("(kc pp) m -> pp kc m", pp=128)
    p.dma("gpsimd", lambda e: e.dma_start(out=wvb[:], in_=wv4))
    p.barrier()
    for mt in range(TC // 128):
        p.dma("sync", lambda e, mt=mt: e.dma_start(
            out=xb[:], in_=xn4[:, :, mt * 128:(mt + 1) * 128]))
        p.barrier()
        for nv in range(4):
            par = nv % 2
            for kc in range(KC):
                p.op("tensor", lambda t, kc=kc, nv=nv, par=par, s=(kc == 0),
                     st=(kc == KC - 1): t.matmul(
                         psv[par][:], xb[:, kc, :], wvb[:, kc, nv * 512:(nv + 1) * 512],
                         start=s, stop=st), inc=(mmv if kc == KC - 1 else None))
            p.wait("vector", mmv)
            p.op("vector", lambda v, par=par: v.tensor_copy(vs[:], psv[par][:]),
                 inc=evv)
            p.wait("sync", evv)
            p.dma("sync", lambda e, mt=mt, nv=nv: e.dma_start(
                out=v_out[mt * 128:(mt + 1) * 128, nv * 512:(nv + 1) * 512],
                in_=vs[:]))
            p.barrier()
    in_maps = [{"xn": np.ascontiguousarray(xn_fm[:, c * TC:(c + 1) * TC]),
                "wq": wq_T, "wk": wk_T, "wv": wv_T} for c in range(NCORES)]
    res = p.run(in_maps)
    q = np.concatenate([r["q"] for r in res], axis=1)
    k = np.concatenate([r["k"] for r in res], axis=1)
    v = np.concatenate([r["v"] for r in res], axis=0)
    return q, k, v



def launch_attention(q_fm, k_fm, v_tm, n_heads_core=8):
    HD = 128
    RH = n_heads_core * HD
    p = Prog("att")
    q_in = p.dram_in("q", [RH, SEQ], BF16)
    k_in = p.dram_in("k", [RH, SEQ], BF16)
    v_in = p.dram_in("v", [SEQ, RH], BF16)
    o_out = p.dram_out("o", [RH, SEQ], BF16)
    scr = p.dram_tmp("mscr", [2, 2], F32)

    qh = p.sbuf("qh", [128, SEQ], BF16)
    kh = p.sbuf("kh", [128, SEQ], BF16)
    vh = p.sbuf("vh", [128, SEQ // 128, 128], BF16)
    sq = p.sbuf("sqh", [128, 512], F32)
    r1 = p.sbuf("r1", [1, SEQ], F32)
    mrow = p.sbuf("mrow", [1, 1], F32)
    mrow2 = p.sbuf("mrow2", [1, 1], F32)
    mprod = p.sbuf("mprod", [1, 1], F32)
    msqrt = p.sbuf("msqrt", [1, 1], F32)
    mneg = p.sbuf("mneg", [1, 1], F32)
    negm = p.sbuf("negm", [128, 1], F32)
    onescol = p.sbuf("onescol", [128, 1], F32)
    onescolb = p.sbuf("onescolb", [128, 1], BF16)
    onesb = p.sbuf("onesb", [1, 128], BF16)
    pt = [p.sbuf(f"pt{i}", [128, 512], BF16) for i in range(2)]
    osb = p.sbuf("osb", [128, 512], F32)
    rsb = p.sbuf("rsb", [128, 512], F32)
    rrow = p.sbuf("rrow", [1, 512], F32)
    rrowb = p.sbuf("rrowb", [1, 512], BF16)
    obf = p.sbuf("obf", [128, 512], BF16)
    ps_st = [p.psum(f"ps_st{i}", [128, 512]) for i in range(2)]
    ps_o = p.psum("ps_o", [128, 512])
    ps_sum = p.psum("ps_sum", [1, 512])
    ps_sq = p.psum("ps_sq", [1, 512])
    ps_r = p.psum("ps_r", [128, 512])
    p.op("gpsimd", lambda g: g.memset(onescol[:], 1.0))
    p.op("gpsimd", lambda g: g.memset(onescolb[:], 1.0))
    p.op("gpsimd", lambda g: g.memset(onesb[:], 1.0))
    SCALE = 1.0 / np.sqrt(128.0)

    mmq = p.sem("mmq")
    red = p.sem("red")
    mms = p.sem("mms")
    expd = p.sem("expd")
    pvd = p.sem("pvd")
    od = p.sem("od")

    def head_max_norm(src_tile, out_row):
        for nb in range(SEQ // 512):
            sl = slice(nb * 512, (nb + 1) * 512)
            p.wait("scalar", mmq)
            p.op("scalar", lambda s, sl=sl: s.activation(sq[:], src_tile[:, sl],
                                                         AF.Square), inc=red)
            p.wait("tensor", red)
            p.op("tensor", lambda t, sl=sl: t.matmul(
                ps_sq[:], onescol[:], sq[:], start=True, stop=True), inc=mmq)
            p.wait("vector", mmq)
            p.op("vector", lambda v, sl=sl: v.tensor_copy(r1[:, sl], ps_sq[:]),
                 inc=red)
        p.wait("vector", red)
        p.op("vector", lambda v: v.reduce_max(out_row[:], r1[:], axis=AX.X), inc=red)
        p.wait("vector", red)

    for h in range(n_heads_core):
        p.dma("sync", lambda e, h=h: e.dma_start(
            out=qh[:], in_=q_in[h * 128:(h + 1) * 128, :]))
        p.dma("sync", lambda e, h=h: e.dma_start(
            out=kh[:], in_=k_in[h * 128:(h + 1) * 128, :]))
        p.dma("gpsimd", lambda e, h=h: e.dma_start(
            out=vh[:], in_=v_in.rearrange("(tc pp) d -> pp tc d", pp=128)
            [:, :, h * 128:(h + 1) * 128]))
        p.barrier()
        head_max_norm(qh, mrow)
        head_max_norm(kh, mrow2)
        fin = p.sem("fin")
        p.op("vector", lambda v: v.tensor_tensor(mprod[:], mrow[:], mrow2[:], ALU.mult),
             inc=fin)
        p.wait("scalar", fin)
        p.op("scalar", lambda s: s.activation(msqrt[:], mprod[:], AF.Sqrt), inc=fin)
        p.wait("vector", fin)
        p.op("vector", lambda v: v.tensor_scalar(mneg[:], msqrt[:], -SCALE, None,
                                                 ALU.mult), inc=fin)
        p.barrier()
        p.dma("sync", lambda e: e.dma_start(out=scr[0:1, 0:1], in_=mneg[:1, :1]))
        p.barrier()
        p.dma("sync", lambda e: e.dma_start(
            out=negm[:], in_=scr[0:1, 0:1].broadcast_to([128, 1])))
        p.barrier()
        for tqb in range(SEQ // 512):
            qsl = slice(tqb * 512, (tqb + 1) * 512)
            NKC = SEQ // 128
            for tkc in range(NKC):
                par = tkc % 2
                if tkc >= 2:
                    p.wait("tensor", expd, val=expd.n - 1)
                p.op("tensor", lambda t, tkc=tkc, par=par, qsl=qsl: t.matmul(
                    ps_st[par][:], kh[:, tkc * 128:(tkc + 1) * 128], qh[:, qsl],
                    start=True, stop=True), inc=mms)
                p.wait("scalar", mms)
                if tkc >= 2:
                    p.wait("scalar", pvd, val=pvd.n - 1)
                p.op("scalar", lambda s, par=par: s.activation(
                    pt[par][:], ps_st[par][:], AF.Exp, bias=negm[:], scale=SCALE),
                    inc=expd)
                p.wait("tensor", expd)
                p.op("tensor", lambda t, tkc=tkc, par=par, h=h: t.matmul(
                    ps_o[:], vh[:, tkc, :], pt[par][:],
                    start=(tkc == 0), stop=(tkc == NKC - 1)))
                p.op("tensor", lambda t, tkc=tkc, par=par: t.matmul(
                    ps_sum[:], onescolb[:], pt[par][:],
                    start=(tkc == 0), stop=(tkc == NKC - 1)), inc=pvd)
            p.wait("vector", pvd)
            p.op("vector", lambda v: v.reciprocal(rrow[:], ps_sum[:]))
            p.op("vector", lambda v: v.tensor_copy(rrowb[:], rrow[:]), inc=od)
            p.wait("tensor", od)
            p.op("tensor", lambda t: t.matmul(ps_r[:], onesb[:1, :],
                                              rrowb[:1, :], start=True, stop=True),
                 inc=od)
            p.wait("vector", od)
            p.op("vector", lambda v: v.tensor_copy(osb[:], ps_o[:]))
            p.op("vector", lambda v: v.tensor_copy(rsb[:], ps_r[:]))
            p.op("vector", lambda v: v.tensor_tensor(obf[:], osb[:], rsb[:], ALU.mult),
                 inc=od)
            p.wait("sync", od)
            p.dma("sync", lambda e, h=h, tqb=tqb: e.dma_start(
                out=o_out[h * 128:(h + 1) * 128, tqb * 512:(tqb + 1) * 512],
                in_=obf[:]))
            p.barrier()

    in_maps = []
    for c in range(NCORES):
        b, hg = c // 2, c % 2
        rows = slice(hg * RH, (hg + 1) * RH)
        cols = slice(b * SEQ, (b + 1) * SEQ)
        in_maps.append({
            "q": np.ascontiguousarray(q_fm[rows, cols]),
            "k": np.ascontiguousarray(k_fm[rows, cols]),
            "v": np.ascontiguousarray(v_tm[cols, rows])})
    res = p.run(in_maps)
    o_fm = np.zeros((D, TOK), dtype=bfdt)
    for c in range(NCORES):
        b, hg = c // 2, c % 2
        o_fm[hg * RH:(hg + 1) * RH, b * SEQ:(b + 1) * SEQ] = res[c]["o"]
    return o_fm



def launch_lin_attn(xn_fm, w_qkv):
    p = Prog("lina")
    RH = 1024
    xn_in = p.dram_in("xn", [D, SEQ], BF16)
    wq_in = p.dram_in("wq", [D, RH], F32)
    wk_in = p.dram_in("wk", [D, RH], F32)
    wv_in = p.dram_in("wv", [D, RH], F32)
    bm_in = p.dram_in("bm", [4, 128], BF16)
    o_out = p.dram_out("o", [RH, SEQ], BF16)
    vkdram = p.dram_tmp("vkd", [32, 1024], BF16)
    ksdram = p.dram_tmp("ksd", [8, 128], F32)

    KC = D // 128
    wq = p.sbuf("wq", [128, KC, RH], BF16)
    wk = p.sbuf("wk", [128, KC, RH], BF16)
    wv = p.sbuf("wv", [128, KC, RH], BF16)
    bm = p.sbuf("bm", [4, 128], BF16)
    xnb = p.sbuf("xnb", [128, KC, 512], BF16)
    kb = p.sbuf("kb", [128, 4, RH], BF16)
    vb = p.sbuf("vb", [128, 4, RH], BF16)
    vk_sb = p.sbuf("vk_sb", [32, 1024], BF16)
    ks_sb = p.sbuf("ks_sb", [1, 1024], F32)
    bd = [p.sbuf(f"bd{g}", [128, 128], BF16) for g in range(8)]
    ksb = [p.sbuf(f"ksb{g}", [128, 4], BF16) for g in range(8)]
    qg = p.sbuf("qg", [128, 512], BF16)
    den = p.sbuf("den", [4, 512], F32)
    denb = p.sbuf("denb", [4, 512], BF16)
    rb = p.sbuf("rb", [128, 512], F32)
    qt = p.sbuf("qt", [128, 512], BF16)
    ob = p.sbuf("ob", [128, 512], BF16)
    ps_kv = [p.psum(f"ps_kv{i}", [128, 512]) for i in range(2)]
    ps_vk = p.psum("ps_vk", [32, 1024])
    ps_ks = p.psum("ps_ks", [1, 1024])
    ps_d = p.psum("ps_d", [4, 512])
    ps_r = p.psum("ps_r", [128, 512])
    ps_q = ps_kv[0]
    ps_of = ps_kv[1]
    onescolb = p.sbuf("onescolb", [128, 1], BF16)
    p.op("gpsimd", lambda g: g.memset(onescolb[:], 1.0))

    mmd = p.sem("mmd")
    ev = p.sem("ev")
    xn4 = xn_in.rearrange("(kc pp) t -> pp kc t", pp=128)
    p.dma("gpsimd", lambda e: e.dma_start(
        out=wq[:], in_=wq_in.rearrange("(kc pp) m -> pp kc m", pp=128)))
    p.dma("gpsimd", lambda e: e.dma_start(
        out=wk[:], in_=wk_in.rearrange("(kc pp) m -> pp kc m", pp=128)))
    p.dma("gpsimd", lambda e: e.dma_start(
        out=wv[:], in_=wv_in.rearrange("(kc pp) m -> pp kc m", pp=128)))
    p.dma("sync", lambda e: e.dma_start(out=bm[:], in_=bm_in))
    p.barrier()

    NTB = SEQ // 512
    for tb in range(NTB):
        p.dma("sync", lambda e, tb=tb: e.dma_start(
            out=xnb[:], in_=xn4[:, :, tb * 512:(tb + 1) * 512]))
        p.barrier()
        for tt in range(4):
            for src_w, dst, do_relu in ((wk, kb, True), (wv, vb, False)):
                for ms in range(2):
                    par = ms % 2
                    msl = slice(ms * 512, (ms + 1) * 512)
                    for kc in range(KC):
                        p.op("tensor", lambda t, kc=kc, tt=tt, msl=msl, par=par,
                             w=src_w, s=(kc == 0), st=(kc == KC - 1): t.matmul(
                                 ps_kv[par][:], xnb[:, kc, tt * 128:(tt + 1) * 128],
                                 w[:, kc, msl], start=s, stop=st),
                             inc=(mmd if kc == KC - 1 else None))
                    p.wait("scalar", mmd)
                    if do_relu:
                        p.op("scalar", lambda s, par=par, tt=tt, msl=msl, d=dst:
                             s.activation(d[:, tt, msl], ps_kv[par][:], AF.Relu),
                             inc=ev)
                    else:
                        p.op("scalar", lambda s, par=par, tt=tt, msl=msl, d=dst:
                             s.activation(d[:, tt, msl], ps_kv[par][:], AF.Identity),
                             inc=ev)
                    p.wait("tensor", ev)
            last = (tb == NTB - 1 and tt == 3)
            for h in range(32):
                hsl = slice(h * 32, (h + 1) * 32)
                first = (tb == 0 and tt == 0 and h % 16 == 0)
                p.op("tensor", lambda t, tt=tt, hsl=hsl, f=first, l=last: t.matmul(
                    ps_vk[:, hsl], kb[:, tt, hsl], vb[:, tt, hsl],
                    start=f, stop=l))
                p.op("tensor", lambda t, tt=tt, hsl=hsl, f=first, l=last: t.matmul(
                    ps_ks[:, hsl], onescolb[:], kb[:, tt, hsl],
                    start=f, stop=l), inc=mmd)
        p.barrier()
    p.op("vector", lambda v: v.tensor_copy(vk_sb[:], ps_vk[:]))
    p.op("vector", lambda v: v.tensor_copy(ks_sb[:], ps_ks[:]), inc=ev)
    p.wait("sync", ev)
    p.dma("sync", lambda e: e.dma_start(out=vkdram, in_=vk_sb[:]))
    p.dma("sync", lambda e: e.dma_start(
        out=ksdram.rearrange("a b -> (a b)").rearrange("(x y) -> x y", x=1),
        in_=ks_sb[:]))
    for g in range(8):
        p.op("gpsimd", lambda gg, g=g: gg.memset(bd[g][:], 0.0))
        p.op("gpsimd", lambda gg, g=g: gg.memset(ksb[g][:], 0.0))
    p.barrier()
    for g in range(8):
        for j in range(4):
            h = 4 * g + j
            p.dma("sync", lambda e, g=g, j=j, h=h: e.dma_start(
                out=bd[g][32 * j:32 * j + 32, 32 * j:32 * j + 32],
                in_=vkdram[0:32, h * 32:(h + 1) * 32]))
            p.dma("gpsimd", lambda e, g=g, j=j, h=h: e.dma_start(
                out=ksb[g][32 * j:32 * j + 32, j:j + 1],
                in_=ksdram.rearrange("a b -> (a b)").rearrange(
                    "(x y) -> x y", y=1)[h * 32:(h + 1) * 32, :]))
    p.barrier()
    for tb in range(NTB):
        p.dma("sync", lambda e, tb=tb: e.dma_start(
            out=xnb[:], in_=xn4[:, :, tb * 512:(tb + 1) * 512]))
        p.barrier()
        for g in range(8):
            gsl = slice(g * 128, (g + 1) * 128)
            for kc in range(KC):
                p.op("tensor", lambda t, kc=kc, gsl=gsl, s=(kc == 0),
                     st=(kc == KC - 1): t.matmul(
                         ps_q[:], wq[:, kc, gsl], xnb[:, kc, :], start=s, stop=st),
                     inc=(mmd if kc == KC - 1 else None))
            p.wait("scalar", mmd)
            p.op("scalar", lambda s: s.activation(qg[:], ps_q[:], AF.Relu), inc=ev)
            p.wait("tensor", ev)
            p.op("tensor", lambda t, g=g: t.matmul(
                ps_d[:], ksb[g][:], qg[:], start=True, stop=True), inc=mmd)
            p.wait("vector", mmd)
            p.op("vector", lambda v: v.tensor_scalar(den[:], ps_d[:], 1e-15, None,
                                                     ALU.add), inc=ev)
            p.wait("vector", ev)
            p.op("vector", lambda v: v.reciprocal(den[:], den[:]), inc=ev)
            p.wait("vector", ev)
            p.op("vector", lambda v: v.tensor_copy(denb[:], den[:]), inc=ev)
            p.wait("tensor", ev)
            p.op("tensor", lambda t: t.matmul(ps_r[:], bm[:], denb[:],
                                              start=True, stop=True), inc=mmd)
            p.wait("vector", mmd)
            p.op("vector", lambda v: v.tensor_copy(rb[:], ps_r[:]), inc=ev)
            p.wait("vector", ev)
            p.op("vector", lambda v: v.tensor_tensor(qt[:], qg[:], rb[:], ALU.mult),
                 inc=ev)
            p.wait("tensor", ev)
            p.op("tensor", lambda t, g=g: t.matmul(
                ps_of[:], bd[g][:], qt[:], start=True, stop=True), inc=mmd)
            p.wait("vector", mmd)
            p.op("vector", lambda v: v.tensor_copy(ob[:], ps_of[:]), inc=ev)
            p.wait("sync", ev)
            p.dma("sync", lambda e, g=g, tb=tb: e.dma_start(
                out=o_out[g * 128:(g + 1) * 128, tb * 512:(tb + 1) * 512],
                in_=ob[:]))
            p.barrier()

    bm_host = np.zeros((4, 128), dtype=bfdt)
    for j in range(4):
        bm_host[j, 32 * j:32 * j + 32] = 1.0
    in_maps = []
    for c in range(NCORES):
        b, hg = c // 2, c % 2
        cols = slice(b * SEQ, (b + 1) * SEQ)
        wq_s = np.ascontiguousarray(w_qkv[hg * RH:(hg + 1) * RH, :].T)
        wk_s = np.ascontiguousarray(w_qkv[D + hg * RH:D + (hg + 1) * RH, :].T)
        wv_s = np.ascontiguousarray(w_qkv[2 * D + hg * RH:2 * D + (hg + 1) * RH, :].T)
        in_maps.append({"xn": np.ascontiguousarray(xn_fm[:, cols]),
                        "wq": wq_s, "wk": wk_s, "wv": wv_s, "bm": bm_host})
    res = p.run(in_maps)
    o_fm = np.zeros((D, TOK), dtype=bfdt)
    for c in range(NCORES):
        b, hg = c // 2, c % 2
        o_fm[hg * RH:(hg + 1) * RH, b * SEQ:(b + 1) * SEQ] = res[c]["o"]
    return o_fm



def launch_fp4_dequant(gu_u8, gu_scalesT, dn_u8, dn_scalesT):
    RG = 11264 // NCORES
    RD = 2048 // NCORES
    p = Prog("fp4")
    g_in = p.dram_in("g", [RG, 1024], U8)
    gs_in = p.dram_in("gs", [RG, 16], F16)
    d_in = p.dram_in("d", [RD, 2816], U8)
    ds_in = p.dram_in("ds", [RD, 44], F16)
    g_out = p.dram_out("go", [RG, 2048], BF16)
    d_out = p.dram_out("do", [RD, 5632], BF16)

    W = 2816
    ut = p.sbuf("ut", [128, 1408], U8)
    ui = p.sbuf("ui", [128, 1408], I32)
    nib = p.sbuf("nib", [128, W], I32)
    nf = p.sbuf("nf", [128, W], F32)
    m8i = p.sbuf("m8i", [128, W], I32)
    m8 = p.sbuf("m8", [128, W], F32)
    b0i = p.sbuf("b0i", [128, W], I32)
    b0 = p.sbuf("b0", [128, W], F32)
    e_t = p.sbuf("e_t", [128, W], F32)
    t1 = p.sbuf("t1", [128, W], F32)
    t2 = p.sbuf("t2", [128, W], F32)
    sgn = p.sbuf("sgn", [128, W], F32)
    msk = p.sbuf("msk", [128, W], F32)
    aa = p.sbuf("aa", [128, W], F32)
    bb2 = p.sbuf("bb2", [128, W], F32)
    val = p.sbuf("val", [128, W], F32)
    scf = p.sbuf("scf", [128, 22], F32)
    sce = p.sbuf("sce", [128, W], F32)
    ones128 = p.sbuf("ones128", [128, 128], F32)
    outb = p.sbuf("outb", [128, W], BF16)
    p.op("gpsimd", lambda g: g.memset(ones128[:], 1.0))

    def dq(src, ssrc, dst, nrows, nwords, ngroups):
        nt = (nrows + 127) // 128
        done = p.sem("dq")
        WB = 1408
        for i in range(nt):
          r0 = i * 128
          pr = min(nrows, r0 + 128) - r0
          for wb0 in range(0, nwords, WB):
            wcnt = min(WB, nwords - wb0)
            ncols = wcnt * 2
            g0 = (wb0 * 2) // 128
            ng = ncols // 128
            p.dma("sync", lambda e, r0=r0, pr=pr, wb0=wb0, wcnt=wcnt: e.dma_start(
                out=ut[:pr, :wcnt], in_=src[r0:r0 + pr, wb0:wb0 + wcnt]))
            p.dma("gpsimd", lambda e, r0=r0, pr=pr, g0=g0, ng=ng: e.dma_start(
                out=scf[:pr, :ng], in_=ssrc[r0:r0 + pr, g0:g0 + ng]))
            p.barrier()

            def V(fn):
                p.op("vector", fn)
            V(lambda v: v.tensor_copy(ui[:pr, :wcnt], ut[:pr, :wcnt]))
            V(lambda v: v.tensor_scalar(nib[:pr, 0:ncols:2], ui[:pr, :wcnt],
                                        15, None, ALU.bitwise_and))
            V(lambda v: v.tensor_scalar(nib[:pr, 1:ncols:2], ui[:pr, :wcnt],
                                        4, 15, ALU.logical_shift_right,
                                        ALU.bitwise_and))
            V(lambda v: v.tensor_copy(nf[:pr, :ncols], nib[:pr, :ncols]))
            V(lambda v: v.tensor_scalar(sgn[:pr, :ncols], nf[:pr, :ncols],
                                        8.0, -2.0, ALU.is_ge, ALU.mult))
            V(lambda v: v.tensor_scalar(sgn[:pr, :ncols], sgn[:pr, :ncols],
                                        1.0, None, ALU.add))
            V(lambda v: v.tensor_scalar(m8i[:pr, :ncols], nib[:pr, :ncols],
                                        7, None, ALU.bitwise_and))
            V(lambda v: v.tensor_copy(m8[:pr, :ncols], m8i[:pr, :ncols]))
            V(lambda v: v.tensor_scalar(b0i[:pr, :ncols], nib[:pr, :ncols],
                                        1, None, ALU.bitwise_and))
            V(lambda v: v.tensor_copy(b0[:pr, :ncols], b0i[:pr, :ncols]))
            V(lambda v: v.tensor_tensor(e_t[:pr, :ncols], m8[:pr, :ncols],
                                        b0[:pr, :ncols], ALU.subtract))
            V(lambda v: v.tensor_scalar(e_t[:pr, :ncols], e_t[:pr, :ncols],
                                        0.5, None, ALU.mult))
            V(lambda v: v.tensor_tensor(t1[:pr, :ncols], e_t[:pr, :ncols],
                                        e_t[:pr, :ncols], ALU.mult))
            V(lambda v: v.tensor_scalar(t1[:pr, :ncols], t1[:pr, :ncols],
                                        5.0, None, ALU.add))
            V(lambda v: v.tensor_tensor(t2[:pr, :ncols], t1[:pr, :ncols],
                                        e_t[:pr, :ncols], ALU.mult))
            V(lambda v: v.tensor_scalar(t2[:pr, :ncols], t2[:pr, :ncols],
                                        6.0, 1.0 / 12.0, ALU.add, ALU.mult))
            V(lambda v: v.tensor_scalar(aa[:pr, :ncols], b0[:pr, :ncols],
                                        0.5, 1.0, ALU.mult, ALU.add))
            V(lambda v: v.tensor_tensor(aa[:pr, :ncols], aa[:pr, :ncols],
                                        t2[:pr, :ncols], ALU.mult))
            V(lambda v: v.tensor_scalar(bb2[:pr, :ncols], b0[:pr, :ncols],
                                        0.5, None, ALU.mult))
            V(lambda v: v.tensor_scalar(msk[:pr, :ncols], m8[:pr, :ncols],
                                        2.0, None, ALU.is_ge))
            V(lambda v: v.tensor_tensor(aa[:pr, :ncols], aa[:pr, :ncols],
                                        bb2[:pr, :ncols], ALU.subtract))
            V(lambda v: v.tensor_tensor(val[:pr, :ncols], msk[:pr, :ncols],
                                        aa[:pr, :ncols], ALU.mult))
            V(lambda v: v.tensor_tensor(val[:pr, :ncols], val[:pr, :ncols],
                                        bb2[:pr, :ncols], ALU.add))
            V(lambda v: v.tensor_tensor(val[:pr, :ncols], val[:pr, :ncols],
                                        sgn[:pr, :ncols], ALU.mult))
            for gi in range(ng):
                V(lambda v, gi=gi: v.tensor_single_scalar(
                    sce[:pr, gi * 128:(gi + 1) * 128], ones128[:pr, :],
                    scf[:pr, gi:gi + 1], ALU.mult))
            p.op("vector", lambda v: v.tensor_tensor(
                outb[:pr, :ncols], val[:pr, :ncols], sce[:pr, :ncols], ALU.mult),
                inc=done)
            p.wait("sync", done)
            p.dma("sync", lambda e, r0=r0, pr=pr, wb0=wb0, ncols=ncols: e.dma_start(
                out=dst[r0:r0 + pr, 2 * wb0:2 * wb0 + ncols], in_=outb[:pr, :ncols]))
            p.barrier()

    dq(g_in, gs_in, g_out, RG, 1024, 16)
    dq(d_in, ds_in, d_out, RD, 2816, 44)

    in_maps = []
    for c in range(NCORES):
        in_maps.append({
            "g": np.ascontiguousarray(gu_u8[c * RG:(c + 1) * RG, :]),
            "gs": np.ascontiguousarray(gu_scalesT[c * RG:(c + 1) * RG, :]),
            "d": np.ascontiguousarray(dn_u8[c * RD:(c + 1) * RD, :]),
            "ds": np.ascontiguousarray(dn_scalesT[c * RD:(c + 1) * RD, :])})
    res = p.run(in_maps)
    w_gu = np.concatenate([r["go"] for r in res], axis=0)
    w_dn = np.concatenate([r["do"] for r in res], axis=0)
    return w_gu, w_dn



def launch_mamba_proj(xn_fm, in_proj_w, dt_bias):
    p = Prog("mproj")
    KC = D // 128
    xn_in = p.dram_in("xn", [D, TC], BF16)
    w_in = p.dram_in("w", [D, 8512], F32)
    db_in = p.dram_in("db", [64, 1], F32)
    gate_out = p.dram_out("gate", [4096, TC], BF16)
    hbc_out = p.dram_out("hbc", [4352, TC], F32)
    dt_out = p.dram_out("dt", [64, TC], F32)

    hb = p.sbuf("hb", [128, KC, 512], BF16)
    wb = [p.sbuf(f"wb{i}", [128, KC, 128], BF16) for i in range(2)]
    ps = [p.psum(f"ps{i}", [128, 512]) for i in range(2)]
    gb = p.sbuf("gb", [128, 512], BF16)
    fb = p.sbuf("fb", [128, 512], F32)
    dtb = p.sbuf("dtb", [64, 1], F32)
    t_r = p.sbuf("t_r", [64, 512], F32)
    t_a = p.sbuf("t_a", [64, 512], F32)
    t_e = p.sbuf("t_e", [64, 512], F32)
    t_l = p.sbuf("t_l", [64, 512], F32)
    t_b = p.sbuf("t_b", [64, 512], F32)
    mmd = p.sem("mmd")
    ev = p.sem("ev")
    xn4 = xn_in.rearrange("(kc pp) t -> pp kc t", pp=128)
    w4 = w_in.rearrange("(kc pp) m -> pp kc m", pp=128)
    p.dma("sync", lambda e: e.dma_start(out=dtb[:], in_=db_in))
    p.barrier()
    m_tiles = [(mi * 128, 128) for mi in range(66)] + [(8448, 64)]
    for nb in range(4):
        p.dma("sync", lambda e, nb=nb: e.dma_start(
            out=hb[:], in_=xn4[:, :, nb * 512:(nb + 1) * 512]))
        p.dma("gpsimd", lambda e: e.dma_start(out=wb[0][:, :, 0:128],
                                              in_=w4[:, :, 0:128]))
        p.barrier()
        for mi, (m0, msz) in enumerate(m_tiles):
            par = mi % 2
            if mi + 1 < len(m_tiles):
                n0, nsz = m_tiles[mi + 1]
                p.dma("gpsimd", lambda e, n0=n0, nsz=nsz, b=wb[1 - par]: e.dma_start(
                    out=b[:, :, 0:nsz], in_=w4[:, :, n0:n0 + nsz]))
            for kc in range(KC):
                p.op("tensor", lambda t, b=wb[par], kc=kc, s=(kc == 0),
                     st=(kc == KC - 1), pp=ps[par], msz=msz: t.matmul(
                         pp[:msz, :], b[:, kc, 0:msz], hb[:, kc, :],
                         start=s, stop=st), inc=(mmd if kc == KC - 1 else None))
            tsl = slice(nb * 512, (nb + 1) * 512)
            if m0 < 4096:
                p.wait("vector", mmd)
                p.op("vector", lambda v, par=par: v.tensor_copy(gb[:], ps[par][:]),
                     inc=ev)
                p.wait("sync", ev)
                p.dma("sync", lambda e, m0=m0, tsl=tsl: e.dma_start(
                    out=gate_out[m0:m0 + 128, tsl], in_=gb[:]))
            elif m0 < 8448:
                p.wait("vector", mmd)
                p.op("vector", lambda v, par=par: v.tensor_copy(fb[:], ps[par][:]),
                     inc=ev)
                p.wait("sync", ev)
                p.dma("sync", lambda e, m0=m0, tsl=tsl: e.dma_start(
                    out=hbc_out[m0 - 4096:m0 - 4096 + 128, tsl], in_=fb[:]))
            else:
                p.wait("vector", mmd)
                p.op("vector", lambda v, par=par: v.tensor_scalar(
                    t_b[:], ps[par][:64, :], dtb[:], None, ALU.add))
                p.op("vector", lambda v: v.tensor_scalar(
                    t_r[:], t_b[:], 0.0, None, ALU.max), inc=ev)
                p.wait("scalar", ev)
                p.op("scalar", lambda s: s.activation(t_a[:], t_b[:], AF.Abs))
                p.op("scalar", lambda s: s.activation(t_e[:], t_a[:], AF.Exp,
                                                      scale=-1.0))
                p.op("scalar", lambda s: s.activation(t_l[:], t_e[:], AF.Ln,
                                                      bias=1.0), inc=ev)
                p.wait("vector", ev)
                p.op("vector", lambda v: v.tensor_tensor(t_b[:], t_r[:], t_l[:],
                                                         ALU.add), inc=ev)
                p.wait("sync", ev)
                p.dma("sync", lambda e, tsl=tsl: e.dma_start(
                    out=dt_out[:, tsl], in_=t_b[:]))
            p.barrier()
    in_maps = [{"xn": np.ascontiguousarray(xn_fm[:, c * TC:(c + 1) * TC]),
                "w": in_proj_w,
                "db": dt_bias.reshape(64, 1).astype(np.float32)}
               for c in range(NCORES)]
    res = p.run(in_maps)
    gate = np.concatenate([r["gate"] for r in res], axis=1)
    hbc = np.concatenate([r["hbc"] for r in res], axis=1)
    dt = np.concatenate([r["dt"] for r in res], axis=1)
    return gate, hbc, dt



def launch_conv(hbc_halo, conv_w, conv_b):
    p = Prog("conv")
    CH = 4352
    h_in = p.dram_in("h", [CH, TC + 3], F32)
    w_in = p.dram_in("w", [CH, 4], F32)
    b_in = p.dram_in("b", [CH, 1], F32)
    o_out = p.dram_out("o", [CH, TC], F32)
    xt = p.sbuf("xt", [128, TC + 3], F32)
    wt = p.sbuf("wt", [128, 4], F32)
    bt = p.sbuf("bt", [128, 1], F32)
    acc = p.sbuf("acc", [128, TC], F32)
    ob = p.sbuf("ob", [128, TC], F32)
    ev = p.sem("ev")
    for ci in range(CH // 128):
        r = slice(ci * 128, (ci + 1) * 128)
        p.dma("sync", lambda e, r=r: e.dma_start(out=xt[:], in_=h_in[r, :]))
        p.dma("scalar", lambda e, r=r: e.dma_start(out=wt[:], in_=w_in[r, :]))
        p.dma("scalar", lambda e, r=r: e.dma_start(out=bt[:], in_=b_in[r, :]))
        p.barrier()
        p.op("vector", lambda v: v.tensor_scalar(
            acc[:], xt[:, 0:TC], wt[:, 0:1], None, ALU.mult))
        for k in range(1, 4):
            p.op("vector", lambda v, k=k: v.scalar_tensor_tensor(
                acc[:], xt[:, k:k + TC], wt[:, k:k + 1], acc[:],
                ALU.mult, ALU.add), inc=(ev if k == 3 else None))
        p.wait("scalar", ev)
        p.op("scalar", lambda s: s.activation(ob[:], acc[:], AF.Silu, bias=bt[:]),
             inc=ev)
        p.wait("sync", ev)
        p.dma("sync", lambda e, r=r: e.dma_start(out=o_out[r, :], in_=ob[:]))
        p.barrier()
    in_maps = []
    for c in range(NCORES):
        t0 = c * TC
        if c % 2 == 0:
            halo = np.zeros((CH, 3), np.float32)
        else:
            halo = hbc_halo[:, t0 - 3:t0]
        m = {"h": np.ascontiguousarray(np.concatenate([halo, hbc_halo[:, t0:t0 + TC]], axis=1)),
             "w": conv_w.astype(np.float32),
             "b": conv_b.reshape(CH, 1).astype(np.float32)}
        in_maps.append(m)
    res = p.run(in_maps)
    return np.concatenate([r["o"] for r in res], axis=1)



def launch_ssm(x_tm_host, B_fm_host, C_fm_host, B_tm_host, dt_host, A_log, D_param):
    p = Prog("ssm")
    NH = 32
    x_in = p.dram_in("x", [SEQ, NH * 64], F32)
    bf_in = p.dram_in("bf", [128, SEQ], F32)
    cf_in = p.dram_in("cf", [128, SEQ], F32)
    bt_in = p.dram_in("bt", [SEQ, 128], F32)
    dt_in = p.dram_in("dt", [NH, SEQ], F32)
    al_in = p.dram_in("al", [NH, 1], F32)
    dp_in = p.dram_in("dp", [1, NH], F32)
    y_out = p.dram_out("y", [SEQ, NH * 64], BF16)
    cumdram = p.dram_tmp("cumd", [NH, SEQ], F32)
    sdodram = p.dram_tmp("sdod", [NH, SEQ], F32)
    cddram = p.dram_tmp("cdd", [NH, 16], F32)

    Bf = p.sbuf("Bf", [128, SEQ], F32)
    Cf = p.sbuf("Cf", [128, SEQ], F32)
    Bfb = p.sbuf("Bfb", [128, SEQ], BF16)
    Cfb = p.sbuf("Cfb", [128, SEQ], BF16)
    dt_r = p.sbuf("dt_r", [NH, SEQ], F32)
    dA = p.sbuf("dA", [NH, SEQ], F32)
    cum = p.sbuf("cum", [NH, SEQ], F32)
    wrow = p.sbuf("wrow", [NH, SEQ], F32)
    sdor = p.sbuf("sdor", [NH, SEQ], F32)
    cdr = p.sbuf("cdr", [NH, 16], F32)
    acol = p.sbuf("acol", [NH, 1], F32)
    tmpr = p.sbuf("tmpr", [NH, 256], F32)
    ident = p.sbuf("ident", [128, 128], F32)
    cumT = p.sbuf("cumT", [128, 32, NH], F32)
    dtT = p.sbuf("dtT", [128, 32, NH], F32)
    wT = p.sbuf("wT", [128, 32, NH], F32)
    trsb = p.sbuf("trsb", [128, NH], F32)
    dcol = p.sbuf("dcol", [128, NH], F32)
    cdb = p.sbuf("cdb", [128, NH * 16], F32)
    xb2 = p.sbuf("xb2", [128, 2, 64], BF16)
    bt2 = p.sbuf("bt2", [128, 2, 128], BF16)
    bwb = p.sbuf("bwb", [128, 2, 128], BF16)
    cbt = p.sbuf("cbt", [128, 2, 256], BF16)
    m1 = p.sbuf("m1", [128, 256], F32)
    sdob = p.sbuf("sdob", [128, 256], F32)
    cwb = p.sbuf("cwb", [128, 256], BF16)
    diff = p.sbuf("diff", [128, 256], F32)
    pte = p.sbuf("pte", [128, 256], F32)
    stb = [p.sbuf(f"stb{j}", [128, 256], BF16) for j in range(2)]
    maskC = [p.sbuf(f"maskC{j}", [128, 256], F32) for j in range(2)]
    zz = p.sbuf("zz", [128, 256], F32)
    Ssb = p.sbuf("Ssb", [128, NH, 2, 64], F32)
    Sbf = p.sbuf("Sbf", [128, 64], BF16)
    yhb = p.sbuf("yhb", [128, 64], BF16)
    ps_tr = p.psum("ps_tr", [128, NH])
    ps_cbt = p.psum("ps_cbt", [128, 256])
    ps_y = [p.psum(f"ps_y{i}", [128, 64]) for i in range(2)]
    ps_st = p.psum("ps_st", [128, 64])

    mmd = p.sem("mmd")
    ev = p.sem("ev")
    evd = p.sem("evD")
    scn = p.sem("scn")

    p.op("gpsimd", lambda g: g.memset(ident[:], 0.0))
    p.op("gpsimd", lambda g: g.affine_select(
        out=ident[:], in_=ident[:], compare_op=ALU.not_equal, fill=1.0,
        base=0, pattern=[[-1, 128]], channel_multiplier=1))
    p.op("gpsimd", lambda g: g.memset(zz[:], 0.0))
    for j in range(2):
        p.op("gpsimd", lambda g, j=j: g.affine_select(
            out=maskC[j][:], in_=zz[:], compare_op=ALU.is_ge, fill=-1e30,
            base=-128 * j, pattern=[[1, 256]], channel_multiplier=-1))
    p.dma("sync", lambda e: e.dma_start(out=Bf[:], in_=bf_in))
    p.dma("sync", lambda e: e.dma_start(out=Cf[:], in_=cf_in))
    p.dma("gpsimd", lambda e: e.dma_start(out=Bfb[:], in_=bf_in))
    p.dma("gpsimd", lambda e: e.dma_start(out=Cfb[:], in_=cf_in))
    p.dma("sync", lambda e: e.dma_start(out=dt_r[:], in_=dt_in))
    p.dma("sync", lambda e: e.dma_start(out=acol[:], in_=al_in))
    p.barrier()
    p.op("scalar", lambda s: s.activation(acol[:], acol[:], AF.Exp), inc=ev)
    p.wait("vector", ev)
    p.op("vector", lambda v: v.tensor_scalar(acol[:], acol[:], -1.0, None, ALU.mult),
         inc=ev)
    p.wait("vector", ev)
    p.op("vector", lambda v: v.tensor_scalar(dA[:], dt_r[:], acol[:], None, ALU.mult))
    for c in range(16):
        sl = slice(c * 256, (c + 1) * 256)
        p.op("vector", lambda v, sl=sl: v.tensor_tensor_scan(
            cum[:, sl], dA[:, sl], dA[:, sl], 0.0, ALU.add, ALU.bypass),
            inc=scn)
    p.wait("vector", scn)
    p.op("vector", lambda v: v.tensor_copy(cdr[:], cum[:, 255::256]))
    for c in range(16):
        sl = slice(c * 256, (c + 1) * 256)
        p.op("vector", lambda v, sl=sl, c=c: v.tensor_scalar(
            wrow[:, sl], cum[:, sl], cum[:, c * 256 + 255:c * 256 + 256], None,
            ALU.subtract), inc=(ev if c == 15 else None))
    p.wait("scalar", ev)
    p.op("scalar", lambda s: s.activation(sdor[:], cum[:], AF.Exp))
    p.op("scalar", lambda s: s.activation(wrow[:], wrow[:], AF.Exp, scale=-1.0))
    p.op("scalar", lambda s: s.activation(cdr[:], cdr[:], AF.Exp), inc=ev)
    p.wait("vector", ev)
    p.op("vector", lambda v: v.tensor_tensor(wrow[:], wrow[:], dt_r[:], ALU.mult),
         inc=ev)
    p.barrier()
    p.dma("sync", lambda e: e.dma_start(out=cumdram, in_=cum[:]))
    p.dma("sync", lambda e: e.dma_start(out=sdodram, in_=sdor[:]))
    p.dma("sync", lambda e: e.dma_start(out=cddram, in_=cdr[:]))
    p.barrier()
    p.dma("sync", lambda e: e.dma_start(
        out=cdb[:], in_=cddram.rearrange("a b -> (a b)").rearrange(
            "(x y) -> x y", x=1).broadcast_to([128, NH * 16])))
    p.dma("gpsimd", lambda e: e.dma_start(
        out=dcol[:], in_=dp_in.broadcast_to([128, NH])))
    for (rows, dstT) in ((cum, cumT), (dt_r, dtT), (wrow, wT)):
        for t in range(32):
            p.op("tensor", lambda tt, rows=rows, t=t: tt.transpose(
                ps_tr[:], rows[:, t * 128:(t + 1) * 128], ident[0:NH, 0:NH]),
                inc=mmd)
            p.wait("vector", mmd)
            p.op("vector", lambda v, dstT=dstT, t=t: v.tensor_copy(
                dstT[:, t, :], ps_tr[:]), inc=ev)
            p.wait("tensor", ev)
    p.op("gpsimd", lambda g: g.memset(Ssb[:], 0.0))
    p.barrier()

    for c in range(16):
        csl = slice(c * 256, (c + 1) * 256)
        for jt in range(2):
            p.op("tensor", lambda t, c=c, jt=jt, csl=csl: t.matmul(
                ps_cbt[:], Bfb[:, c * 256 + jt * 128:c * 256 + (jt + 1) * 128],
                Cfb[:, csl], start=True, stop=True), inc=mmd)
            p.wait("vector", mmd)
            p.op("vector", lambda v, jt=jt: v.tensor_copy(cbt[:, jt, :], ps_cbt[:]),
                 inc=ev)
            p.wait("tensor", ev)
        p.dma("gpsimd", lambda e, c=c: e.dma_start(
            out=bt2[:], in_=bt_in.rearrange("(cc lt pp) n -> pp (cc lt) n", pp=128, lt=2)
            [:, 2 * c:2 * c + 2, :]))
        p.barrier()
        for h in range(NH):
            par = c % 2
            p.dma("gpsimd", lambda e, c=c, h=h: e.dma_start(
                out=xb2[:], in_=x_in.rearrange("(cc lt pp) d -> pp (cc lt) d",
                                               pp=128, lt=2)
                [:, 2 * c:2 * c + 2, h * 64:(h + 1) * 64]))
            p.dma("sync", lambda e, c=c, h=h: e.dma_start(
                out=m1[:], in_=cumdram[h:h + 1, c * 256:(c + 1) * 256]
                .broadcast_to([128, 256])))
            p.dma("scalar", lambda e, c=c, h=h: e.dma_start(
                out=sdob[:], in_=sdodram[h:h + 1, c * 256:(c + 1) * 256]
                .broadcast_to([128, 256])))
            p.barrier()
            p.op("vector", lambda v, h=h, par=par: v.tensor_copy(
                Sbf[:], Ssb[:, h, par, :]))
            p.op("vector", lambda v, csl=csl: v.tensor_tensor(
                cwb[:], Cf[:, csl], sdob[:], ALU.mult))
            for jt in range(2):
                tile = 2 * c + jt
                p.op("vector", lambda v, jt=jt, tile=tile, h=h: v.tensor_scalar(
                    diff[:], m1[:], cumT[:, tile, h:h + 1], None, ALU.subtract))
                p.op("vector", lambda v, jt=jt: v.tensor_tensor(
                    diff[:], diff[:], maskC[jt][:], ALU.min), inc=ev)
                p.wait("scalar", ev)
                p.op("scalar", lambda s, jt=jt: s.activation(pte[:], diff[:], AF.Exp),
                     inc=ev)
                p.wait("vector", ev)
                p.op("vector", lambda v, jt=jt, tile=tile, h=h: v.tensor_scalar(
                    pte[:], pte[:], dtT[:, tile, h:h + 1], None, ALU.mult))
                p.op("vector", lambda v, jt=jt: v.tensor_tensor(
                    stb[jt][:], pte[:], cbt[:, jt, :], ALU.mult))
                p.op("vector", lambda v, jt=jt, tile=tile, h=h: v.tensor_scalar(
                    bwb[:, jt, :], bt2[:, jt, :], wT[:, tile, h:h + 1], None,
                    ALU.mult))
            p.barrier()
            for it in range(2):
                for jt in range(2):
                    p.op("tensor", lambda t, it=it, jt=jt: t.matmul(
                        ps_y[it][:], stb[jt][:, it * 128:(it + 1) * 128],
                        xb2[:, jt, :], start=(jt == 0), stop=False))
                p.op("tensor", lambda t, it=it: t.matmul(
                    ps_y[it][:], cwb[:, it * 128:(it + 1) * 128], Sbf[:],
                    start=False, stop=True))
            for jt in range(2):
                p.op("tensor", lambda t, jt=jt: t.matmul(
                    ps_st[:], bwb[:, jt, :], xb2[:, jt, :], start=(jt == 0),
                    stop=(jt == 1)))
            p.barrier()
            for it in range(2):
                p.op("vector", lambda v, it=it, h=h: v.scalar_tensor_tensor(
                    yhb[:], xb2[:, it, :], dcol[:, h:h + 1], ps_y[it][:],
                    ALU.mult, ALU.add), inc=evd)
                p.wait("sync", evd)
                p.dma("sync", lambda e, c=c, it=it, h=h: e.dma_start(
                    out=y_out[c * 256 + it * 128:c * 256 + (it + 1) * 128,
                              h * 64:(h + 1) * 64], in_=yhb[:]))
                p.barrier()
            p.op("vector", lambda v, h=h, par=par, c=c: v.scalar_tensor_tensor(
                Ssb[:, h, 1 - par, :], Ssb[:, h, par, :],
                cdb[:, h * 16 + c:h * 16 + c + 1], ps_st[:],
                ALU.mult, ALU.add))
            p.barrier()

    in_maps = []
    for c in range(NCORES):
        b, hg = c // 2, c % 2
        cols = slice(b * SEQ, (b + 1) * SEQ)
        in_maps.append({
            "x": np.ascontiguousarray(x_tm_host[cols, hg * 2048:(hg + 1) * 2048]),
            "bf": np.ascontiguousarray(B_fm_host[:, cols]),
            "cf": np.ascontiguousarray(C_fm_host[:, cols]),
            "bt": np.ascontiguousarray(B_tm_host[cols, :]),
            "dt": np.ascontiguousarray(dt_host[hg * NH:(hg + 1) * NH, cols]),
            "al": np.ascontiguousarray(A_log[hg * NH:(hg + 1) * NH].reshape(NH, 1).astype(np.float32)),
            "dp": np.ascontiguousarray(D_param[hg * NH:(hg + 1) * NH].reshape(1, NH).astype(np.float32))})
    res = p.run(in_maps)
    yh = np.zeros((TOK, 4096), dtype=bfdt)
    for c in range(NCORES):
        b, hg = c // 2, c % 2
        yh[b * SEQ:(b + 1) * SEQ, hg * 2048:(hg + 1) * 2048] = res[c]["y"]
    return yh



def launch_mamba_gate(yh_fm, gate_fm, norm_w):
    p = Prog("mgate")
    DI = 4096
    y_in = p.dram_in("y", [DI, TC], BF16)
    g_in = p.dram_in("g", [DI, TC], BF16)
    nw_in = p.dram_in("nw", [128, 32], F32)
    h_out = p.dram_out("h", [DI, TC], BF16)
    hid_dram = p.dram_tmp("hid", [DI, TC], F32)

    yb = p.sbuf("yb", [128, TC], BF16)
    gb = p.sbuf("gb", [128, TC], BF16)
    sg = p.sbuf("sg", [128, TC], F32)
    hid = p.sbuf("hid", [128, TC], F32)
    nw = p.sbuf("nw", [128, 32], F32)
    hb = p.sbuf("hbo", [128, TC], BF16)
    tmp = p.sbuf("tmp", [128, TC], F32)
    acc = LNAcc(p, TC, tag="rmsg", want_s1=False)
    ev = p.sem("ev")
    p.dma("scalar", lambda e: e.dma_start(out=nw[:], in_=nw_in))
    p.barrier()
    for ci in range(32):
        r = slice(ci * 128, (ci + 1) * 128)
        p.dma("sync", lambda e, r=r: e.dma_start(out=yb[:], in_=y_in[r, :]))
        p.dma("scalar", lambda e, r=r: e.dma_start(out=gb[:], in_=g_in[r, :]))
        p.barrier()
        p.op("scalar", lambda s: s.activation(sg[:], gb[:], AF.Silu), inc=ev)
        p.wait("vector", ev)
        p.op("vector", lambda v: v.tensor_tensor(hid[:], yb[:], sg[:], ALU.mult),
             inc=ev)
        p.wait("scalar", ev)
        p.wait("tensor", ev)
        for nb in range(4):
            acc.slice_stats(hid[:, nb * 512:(nb + 1) * 512], nb,
                            first=(ci == 0), last=(ci == 31))
        p.wait("sync", ev)
        p.dma("sync", lambda e, ci=ci: e.dma_start(
            out=hid_dram[ci * 128:(ci + 1) * 128, :], in_=hid[:]))
        p.barrier()
    acc.rms_finalize(DI, 1e-5)
    for ci in range(32):
        p.dma("sync", lambda e, ci=ci: e.dma_start(
            out=hid[:], in_=hid_dram[ci * 128:(ci + 1) * 128, :]))
        p.barrier()
        p.op("vector", lambda v: v.tensor_tensor(tmp[:], hid[:], acc.A[:], ALU.mult))
        p.op("vector", lambda v, ci=ci: v.tensor_scalar(
            hb[:], tmp[:], nw[:, ci:ci + 1], None, ALU.mult), inc=ev)
        p.wait("sync", ev)
        p.dma("sync", lambda e, ci=ci: e.dma_start(
            out=h_out[ci * 128:(ci + 1) * 128, :], in_=hb[:]))
        p.barrier()
    in_maps = [{"y": np.ascontiguousarray(yh_fm[:, c * TC:(c + 1) * TC]),
                "g": np.ascontiguousarray(gate_fm[:, c * TC:(c + 1) * TC]),
                "nw": np.ascontiguousarray(norm_w.astype(np.float32).reshape(32, 128).T)}
               for c in range(NCORES)]
    res = p.run(in_maps)
    return np.concatenate([r["h"] for r in res], axis=1)



def kernel(L0_attn_a_x, L0_attn_a_w_qkv, L0_attn_a_w_proj, L0_attn_a_b_proj,
           L0_attn_b_Wq, L0_attn_b_Wk, L0_attn_b_Wv, L0_attn_b_Wo,
           L0_ffn_lut, L0_ffn_gate_idx, L0_ffn_up_idx, L0_ffn_down_idx,
           L1_attn_a_in_proj_w, L1_attn_a_conv_w, L1_attn_a_conv_b,
           L1_attn_a_dt_bias, L1_attn_a_A_log, L1_attn_a_D_param,
           L1_attn_a_norm_w, L1_attn_a_out_proj_w,
           L1_attn_b_w_q, L1_attn_b_w_k, L1_attn_b_w_v, L1_attn_b_w_o,
           L1_ffn_gate_up_packed, L1_ffn_gate_up_scales,
           L1_ffn_down_packed, L1_ffn_down_scales):
    f32 = np.float32
    x0_fm = np.ascontiguousarray(
        np.asarray(L0_attn_a_x, f32).reshape(TOK, D).T)

    wg, wu, wd = launch_ghost_dequant(
        np.ascontiguousarray(np.asarray(L0_ffn_gate_idx).T),
        np.ascontiguousarray(np.asarray(L0_ffn_up_idx).T),
        np.ascontiguousarray(np.asarray(L0_ffn_down_idx).T),
        np.asarray(L0_ffn_lut, f32))
    w_gu_nat, w_dn_nat = launch_fp4_dequant(
        np.ascontiguousarray(np.asarray(L1_ffn_gate_up_packed).view(np.uint8)),
        np.ascontiguousarray(np.asarray(L1_ffn_gate_up_scales).T),
        np.ascontiguousarray(np.asarray(L1_ffn_down_packed).view(np.uint8)),
        np.ascontiguousarray(np.asarray(L1_ffn_down_scales).T))
    wguT = np.ascontiguousarray(w_gu_nat.T)
    wg1 = np.ascontiguousarray(wguT[:, :5632])
    wu1 = np.ascontiguousarray(wguT[:, 5632:])
    wd1 = np.ascontiguousarray(w_dn_nat.T)

    def T(a):
        return np.ascontiguousarray(np.asarray(a, f32).T)

    xn = launch_ln(x0_fm)
    o = launch_lin_attn(xn, np.asarray(L0_attn_a_w_qkv, f32))
    x, xn = launch_proj_res_ln(o, T(L0_attn_a_w_proj),
                               np.asarray(L0_attn_a_b_proj, f32), x0_fm)
    h = launch_ffn_up(xn, wg, wu)
    x, xn = launch_proj_res_ln(h, wd, None, x)
    q, k, v = launch_qkv(xn, T(L0_attn_b_Wq), T(L0_attn_b_Wk), T(L0_attn_b_Wv))
    o = launch_attention(q, k, v)
    x, xn = launch_proj_res_ln(o, T(L0_attn_b_Wo), None, x)
    h = launch_ffn_up(xn, wg, wu)
    x, xn = launch_proj_res_ln(h, wd, None, x)
    gate, hbc, dt = launch_mamba_proj(xn, np.asarray(L1_attn_a_in_proj_w, f32),
                                      np.asarray(L1_attn_a_dt_bias, f32))
    xbc = launch_conv(hbc, np.asarray(L1_attn_a_conv_w, f32),
                      np.asarray(L1_attn_a_conv_b, f32))
    x_tm = np.ascontiguousarray(xbc[:4096].T)
    B_fm = np.ascontiguousarray(xbc[4096:4224])
    C_fm = np.ascontiguousarray(xbc[4224:4352])
    B_tm = np.ascontiguousarray(B_fm.T)
    yh_tm = launch_ssm(x_tm, B_fm, C_fm, B_tm, dt,
                       np.asarray(L1_attn_a_A_log, f32),
                       np.asarray(L1_attn_a_D_param, f32))
    yh_fm = np.ascontiguousarray(yh_tm.T)
    hn = launch_mamba_gate(yh_fm, gate, np.asarray(L1_attn_a_norm_w, f32))
    x, xn = launch_proj_res_ln(hn, np.asarray(L1_attn_a_out_proj_w, f32), None, x)
    h = launch_ffn_up(xn, wg1, wu1)
    x, xn = launch_proj_res_ln(h, wd1, None, x, rms_pre_eps=1e-6)
    q, k, v = launch_qkv(xn, T(L1_attn_b_w_q), T(L1_attn_b_w_k), T(L1_attn_b_w_v))
    o = launch_attention(q, k, v)
    x, xn = launch_proj_res_ln(o, T(L1_attn_b_w_o), None, x)
    h = launch_ffn_up(xn, wg1, wu1)
    x, _ = launch_proj_res_ln(h, wd1, None, x, rms_pre_eps=1e-6, skip_ln=True)
    return np.ascontiguousarray(x.T).reshape(4, SEQ, D).astype(np.float32)


# version 02
# speedup vs baseline: 1.0107x; 1.0107x over previous
import numpy as np
import ml_dtypes
import concourse.bass as bass
import concourse.mybir as mybir
from concourse.bass_utils import run_bass_kernel_spmd

F32 = mybir.dt.float32
BF16 = mybir.dt.bfloat16
F16 = mybir.dt.float16
I32 = mybir.dt.int32
U8 = mybir.dt.uint8
AF = mybir.ActivationFunctionType
ALU = mybir.AluOpType
AX = mybir.AxisListType
bfdt = ml_dtypes.bfloat16

NCORES = 8
D = 2048
TOK = 16384
TC = 2048
SEQ = 4096

ENGS = ("sync", "tensor", "vector", "scalar", "gpsimd")


class Sem:
    __slots__ = ("h", "n")

    def __init__(self, nc, name):
        self.h = nc.alloc_semaphore(name=name)
        self.n = 0


class Prog:

    def __init__(self, name):
        self.nc = bass.Bass("TRN2", target_bir_lowering=False)
        self.name = name
        self._uid = 0
        self.phases = [{e: [] for e in ENGS}]
        self.dmasem = {e: Sem(self.nc, f"dma_{e}") for e in ENGS}
        self._phase_dma = [{e: 0 for e in ENGS}]
        self.bar = self.nc.alloc_semaphore(name="bar")

    def uid(self):
        self._uid += 1
        return self._uid

    def dram_in(self, name, shape, dtype):
        return self.nc.dram_tensor(name, list(shape), dtype, kind="ExternalInput").ap()

    def dram_out(self, name, shape, dtype):
        return self.nc.dram_tensor(name, list(shape), dtype, kind="ExternalOutput").ap()

    def dram_tmp(self, name, shape, dtype):
        return self.nc.dram_tensor(f"{name}{self.uid()}", list(shape), dtype,
                                   kind="Internal").ap()

    def sbuf(self, name, shape, dtype):
        return self.nc.alloc_sbuf_tensor(f"{name}_{self.uid()}", list(shape), dtype)

    def psum(self, name, shape, dtype=F32):
        return self.nc.alloc_psum_tensor(f"{name}_{self.uid()}", list(shape), dtype)

    def sem(self, name):
        return Sem(self.nc, f"{name}_{self.uid()}")

    def op(self, eng, fn, inc=None, inc_by=1):
        if inc is not None:
            inc.n += inc_by
            self.phases[-1][eng].append(
                ("comp", lambda e, fn=fn, s=inc.h, b=inc_by: fn(e).then_inc(s, b),
                 (inc.h, inc.n)))
        else:
            self.phases[-1][eng].append(("comp", fn, None))

    def dma(self, eng, fn):
        s = self.dmasem[eng]
        s.n += 16
        self.phases[-1][eng].append(
            ("dma", lambda e, fn=fn, h=s.h: fn(e).then_inc(h, 16), None))

    def wait(self, eng, sem, val=None):
        v = sem.n if val is None else val
        self.phases[-1][eng].append(
            ("ctl", lambda e, h=sem.h, v=v: e.wait_ge(h, v), None))

    def wait_dma(self, eng, src_eng=None):
        s = self.dmasem[src_eng or eng]
        self.phases[-1][eng].append(
            ("ctl", lambda e, h=s.h, v=s.n: e.wait_ge(h, v), None))

    def barrier(self):
        self._phase_dma.append({e: self.dmasem[e].n for e in ENGS})
        self.phases.append({e: [] for e in ENGS})

    def _emit(self):
        self._phase_dma.append({e: self.dmasem[e].n for e in ENGS})
        contrib = []
        for phase in self.phases:
            c = {}
            for e in ENGS:
                has_comp = any(k == "comp" for k, *_ in phase[e])
                has_dma = any(k == "dma" for k, *_ in phase[e])
                c[e] = max((1 if has_comp else 0) + (1 if has_dma else 0), 1)
            contrib.append(c)
        totals = np.cumsum([sum(c.values()) for c in contrib]).tolist()

        nc = self.nc
        with nc.Block() as block:
            for ename in ENGS:
                def body(eng, ename=ename):
                    for i, phase in enumerate(self.phases):
                        if i > 0:
                            eng.wait_ge(self.bar, totals[i - 1])
                        items = phase[ename]
                        last_comp_idx = max(
                            (j for j, (k, *_) in enumerate(items) if k == "comp"),
                            default=None)
                        had_comp = last_comp_idx is not None
                        had_dma = any(k == "dma" for k, *_ in items)
                        tail_wait = None
                        for j, (kind, fn, inc_info) in enumerate(items):
                            inst = fn(eng)
                            if kind == "comp" and j == last_comp_idx:
                                if inc_info is None:
                                    inst.then_inc(self.bar, 1)
                                else:
                                    tail_wait = inc_info
                        if tail_wait is not None:
                            eng.wait_ge(tail_wait[0], tail_wait[1])
                            eng.nop().then_inc(self.bar, 1)
                        if had_dma:
                            eng.wait_ge(self.dmasem[ename].h,
                                        self._phase_dma[i + 1][ename])
                            eng.nop().then_inc(self.bar, 1)
                        if not had_comp and not had_dma:
                            eng.nop().then_inc(self.bar, 1)
                getattr(block, ename)(body)

    def run(self, in_maps, core_ids=None):
        self._emit()
        import os as _os
        trace = bool(_os.environ.get("KTRACE"))
        res = run_bass_kernel_spmd(
            self.nc, in_maps,
            core_ids=list(range(NCORES)) if core_ids is None else core_ids,
            trace=trace)
        if trace:
            print(f"[KTRACE] {self.name}: exec_time_ns={res.exec_time_ns}",
                  flush=True)
        return res.results


def pack_w(w):
    K_in, M = w.shape
    KC, MT = K_in // 128, M // 128
    return np.ascontiguousarray(
        w.reshape(KC, 128, MT, 128).transpose(2, 1, 0, 3))



def _ln_shared(p, t_cols):
    sh = p.__dict__.setdefault("_ln_shared_pool", {})
    if t_cols not in sh:
        d = {}
        d["s1"] = p.sbuf("lns_s1", [1, t_cols], F32)
        d["s2"] = p.sbuf("lns_s2", [1, t_cols], F32)
        d["ps1"] = p.psum("lns_ps1", [1, 512])
        d["ps2"] = p.psum("lns_ps2", [1, 512])
        d["ones"] = p.sbuf("lns_ones", [128, 1], F32)
        d["xsq"] = p.sbuf("lns_xsq", [128, 512], F32)
        d["A"] = p.sbuf("lns_A", [128, t_cols], F32)
        d["B"] = p.sbuf("lns_B", [128, t_cols], F32)
        d["scr"] = p.dram_tmp("lns_scr", [2, t_cols], F32)
        for nm in ("mean", "ex2", "var", "std", "msq", "ar", "br"):
            d[nm] = p.sbuf("lns_" + nm, [1, t_cols], F32)
        p.op("gpsimd", lambda g: g.memset(d["ones"][:], 1.0))
        sh[t_cols] = d
    return sh[t_cols]


class LNAcc:
    def __init__(self, p, t_cols, tag="ln", want_s1=True):
        self.p = p
        self.T = t_cols
        self.want_s1 = want_s1
        d = _ln_shared(p, t_cols)
        self.d = d
        self.s1 = d["s1"] if want_s1 else None
        self.s2 = d["s2"]
        self.ps1 = d["ps1"] if want_s1 else None
        self.ps2 = d["ps2"]
        self.ones = d["ones"]
        self.xsq = d["xsq"]
        self.A = d["A"]
        self.B = d["B"]
        self.scratch = d["scr"]
        self.mm = p.sem(tag + "_mm")
        self.sq = p.sem(tag + "_sq")
        self.vd = p.sem(tag + "_vd")

    def slice_stats(self, tile_sl, nb, first, last):
        p = self.p
        sl = slice(nb * 512, (nb + 1) * 512)
        p.wait("scalar", self.mm)
        p.op("scalar", lambda s: s.activation(self.xsq[:], tile_sl, AF.Square),
             inc=self.sq)
        p.wait("tensor", self.vd)
        if self.want_s1:
            p.op("tensor", lambda t: t.matmul(
                self.ps1[:], self.ones[:], tile_sl, start=True, stop=True),
                inc=self.mm)
        p.wait("tensor", self.sq)
        p.op("tensor", lambda t: t.matmul(
            self.ps2[:], self.ones[:], self.xsq[:], start=True, stop=True),
            inc=self.mm)
        p.wait("vector", self.mm)
        if first:
            if self.want_s1:
                p.op("vector", lambda v: v.tensor_copy(self.s1[:, sl], self.ps1[:]))
            p.op("vector", lambda v: v.tensor_copy(self.s2[:, sl], self.ps2[:]),
                 inc=self.vd)
        else:
            if self.want_s1:
                p.op("vector", lambda v: v.tensor_tensor(
                    self.s1[:, sl], self.s1[:, sl], self.ps1[:], ALU.add))
            p.op("vector", lambda v: v.tensor_tensor(
                self.s2[:, sl], self.s2[:, sl], self.ps2[:], ALU.add), inc=self.vd)

    def finalize(self, n_feat, eps=1e-5):
        p, T = self.p, self.T
        d = self.d
        mean, ex2, var, std, sq = d["mean"], d["ex2"], d["var"], d["std"], d["msq"]
        arow, brow = d["ar"], d["br"]
        inv = 1.0 / n_feat
        fin = p.sem("lnfin")
        p.op("vector", lambda v: v.tensor_scalar(mean[:], self.s1[:], inv, None, ALU.mult))
        p.op("vector", lambda v: v.tensor_scalar(ex2[:], self.s2[:], inv, None, ALU.mult))
        p.op("vector", lambda v: v.tensor_tensor(sq[:], mean[:], mean[:], ALU.mult))
        p.op("vector", lambda v: v.tensor_tensor(var[:], ex2[:], sq[:], ALU.subtract))
        p.op("vector", lambda v: v.tensor_scalar(var[:], var[:], eps, None, ALU.add),
             inc=fin)
        p.wait("scalar", fin)
        p.op("scalar", lambda s: s.activation(std[:], var[:], AF.Sqrt), inc=fin)
        p.wait("vector", fin)
        p.op("vector", lambda v: v.reciprocal(arow[:], std[:]), inc=fin)
        p.wait("vector", fin)
        p.op("vector", lambda v: v.tensor_tensor(brow[:], mean[:], arow[:], ALU.mult))
        p.op("vector", lambda v: v.tensor_scalar(brow[:], brow[:], -1.0, None, ALU.mult))
        p.barrier()
        p.dma("sync", lambda e: e.dma_start(out=self.scratch[0:1, :], in_=arow[:1, :]))
        p.dma("sync", lambda e: e.dma_start(out=self.scratch[1:2, :], in_=brow[:1, :]))
        p.barrier()
        p.dma("sync", lambda e: e.dma_start(
            out=self.A[:], in_=self.scratch[0:1, :].broadcast_to([128, T])))
        p.dma("sync", lambda e: e.dma_start(
            out=self.B[:], in_=self.scratch[1:2, :].broadcast_to([128, T])))
        p.barrier()

    def rms_finalize(self, n_feat, eps):
        p, T = self.p, self.T
        ms = self.d["mean"]
        rt = self.d["std"]
        fin = p.sem("rmsfin")
        p.op("vector", lambda v: v.tensor_scalar(ms[:], self.s2[:], 1.0 / n_feat,
                                                 eps, ALU.mult, ALU.add), inc=fin)
        p.wait("scalar", fin)
        p.op("scalar", lambda s: s.activation(rt[:], ms[:], AF.Sqrt), inc=fin)
        p.wait("vector", fin)
        p.op("vector", lambda v: v.reciprocal(ms[:], rt[:]), inc=fin)
        p.wait("vector", fin)
        p.barrier()
        p.dma("sync", lambda e: e.dma_start(out=self.scratch[0:1, :], in_=ms[:1, :]))
        p.barrier()
        p.dma("sync", lambda e: e.dma_start(
            out=self.A[:], in_=self.scratch[0:1, :].broadcast_to([128, T])))
        p.barrier()



def emit_proj_stream(p, h_dram, w_dram, K_in, M_total, T, epilogue, tag="pj",
                     nb_list=None, extra_phase_work=None):
    KC = K_in // 128
    NT = T // 512
    MT = M_total // 128
    if nb_list is None:
        nb_list = list(range(NT))
    hb = p.sbuf(f"{tag}_h", [128, KC, 512], BF16)
    wb = [p.sbuf(f"{tag}_w{i}", [128, KC, 128], BF16) for i in range(2)]
    ps = [p.psum(f"{tag}_ps{i}", [128, 512]) for i in range(2)]
    mmd = p.sem(tag + "_mmd")
    h4 = h_dram.rearrange("(kc pp) t -> pp kc t", pp=128)
    for ni, nb in enumerate(nb_list):
        p.dma("sync", lambda e, nb=nb: e.dma_start(
            out=hb[:], in_=h4[:, :, nb * 512:(nb + 1) * 512]))
        p.dma("gpsimd", lambda e: e.dma_start(out=wb[0][:], in_=w_dram[0]))
        p.barrier()
        for mi in range(MT):
            par = mi % 2
            if mi + 1 < MT:
                p.dma("gpsimd", lambda e, mi=mi, b=wb[1 - par]: e.dma_start(
                    out=b[:], in_=w_dram[mi + 1]))
            for kc in range(KC):
                p.op("tensor",
                     lambda t, b=wb[par], kc=kc, s=(kc == 0), st=(kc == KC - 1),
                     pp=ps[par]: t.matmul(pp[:], b[:, kc, :], hb[:, kc, :],
                                          start=s, stop=st),
                     inc=(mmd if kc == KC - 1 else None))
            p.wait("vector", mmd)
            p.wait("scalar", mmd)
            epilogue(mi, nb, ps[par])
            p.barrier()
        if extra_phase_work is not None:
            extra_phase_work(nb)



def launch_ln(x_fm):
    p = Prog("ln0")
    x_in = p.dram_in("x", [D, TC], F32)
    xn_out = p.dram_out("xn", [D, TC], BF16)
    acc = LNAcc(p, TC)
    xa = [p.sbuf(f"xa{i}", [128, TC], F32) for i in range(2)]
    xn_t = [p.sbuf(f"xnt{i}", [128, TC], BF16) for i in range(2)]
    tmp = p.sbuf("tmp", [128, TC], F32)
    p.dma("sync", lambda e: e.dma_start(out=xa[0][:], in_=x_in[0:128, :]))
    p.barrier()
    for ci in range(16):
        par = ci % 2
        if ci + 1 < 16:
            p.dma("sync", lambda e, ci=ci, b=xa[1 - par]: e.dma_start(
                out=b[:], in_=x_in[(ci + 1) * 128:(ci + 2) * 128, :]))
        for nb in range(4):
            acc.slice_stats(xa[par][:, nb * 512:(nb + 1) * 512], nb,
                            first=(ci == 0), last=(ci == 15))
        p.barrier()
    acc.finalize(D)
    p.dma("sync", lambda e: e.dma_start(out=xa[0][:], in_=x_in[0:128, :]))
    p.barrier()
    st = p.sem("st")
    for ci in range(16):
        par = ci % 2
        if ci + 1 < 16:
            p.dma("sync", lambda e, ci=ci, b=xa[1 - par]: e.dma_start(
                out=b[:], in_=x_in[(ci + 1) * 128:(ci + 2) * 128, :]))
        acc.apply = None
        p.op("vector", lambda v, par=par: v.tensor_tensor(
            tmp[:], xa[par][:], acc.A[:], ALU.mult))
        p.op("vector", lambda v, par=par: v.tensor_tensor(
            xn_t[par][:], tmp[:], acc.B[:], ALU.add), inc=st)
        p.wait("scalar", st)
        p.dma("scalar", lambda e, ci=ci, b=xn_t[par]: e.dma_start(
            out=xn_out[ci * 128:(ci + 1) * 128, :], in_=b[:]))
        p.barrier()
    in_maps = [{"x": np.ascontiguousarray(x_fm[:, c * TC:(c + 1) * TC])}
               for c in range(NCORES)]
    res = p.run(in_maps)
    return np.concatenate([r["xn"] for r in res], axis=1)


def _emit_res_ln_write(p, y_dram, x_in, xp_out, xn_out, pre_scale_A=None):
    acc = LNAcc(p, TC, tag="lnT")
    xc = p.sbuf("xc", [128, TC], F32)
    yc = p.sbuf("yc", [128, TC], F32)
    xp_t = [p.sbuf(f"xpt{i}", [128, TC], F32) for i in range(2)]
    xn_t = [p.sbuf(f"xnt{i}", [128, TC], BF16) for i in range(2)]
    tmp = p.sbuf("tmpT", [128, TC], F32)
    ev = p.sem("resln")
    p.barrier()
    for mi in range(16):
        par = mi % 2
        p.dma("sync", lambda e, mi=mi: e.dma_start(
            out=xc[:], in_=x_in[mi * 128:(mi + 1) * 128, :]))
        p.dma("scalar", lambda e, mi=mi: e.dma_start(
            out=yc[:], in_=y_dram[mi * 128:(mi + 1) * 128, :]))
        p.wait_dma("vector", "sync")
        p.wait_dma("vector", "scalar")
        if pre_scale_A is not None:
            p.op("vector", lambda v: v.tensor_tensor(tmp[:], yc[:], pre_scale_A[:],
                                                     ALU.mult))
            p.op("vector", lambda v, par=par: v.tensor_tensor(
                xp_t[par][:], tmp[:], xc[:], ALU.add), inc=ev)
        else:
            p.op("vector", lambda v, par=par: v.tensor_tensor(
                xp_t[par][:], yc[:], xc[:], ALU.add), inc=ev)
        p.wait("scalar", ev)
        p.wait("tensor", ev)
        for nb in range(4):
            acc.slice_stats(xp_t[par][:, nb * 512:(nb + 1) * 512], nb,
                            first=(mi == 0), last=(mi == 15))
        p.wait("sync", ev)
        p.dma("sync", lambda e, mi=mi, b=xp_t[par]: e.dma_start(
            out=xp_out[mi * 128:(mi + 1) * 128, :], in_=b[:]))
        p.barrier()
    if xn_out is None:
        return
    acc.finalize(D)
    st = p.sem("stT")
    for ci in range(16):
        par = ci % 2
        p.dma("sync", lambda e, ci=ci, b=xp_t[par]: e.dma_start(
            out=b[:], in_=xp_out[ci * 128:(ci + 1) * 128, :]))
        p.barrier()
        p.op("vector", lambda v, par=par: v.tensor_tensor(
            tmp[:], xp_t[par][:], acc.A[:], ALU.mult))
        p.op("vector", lambda v, par=par: v.tensor_tensor(
            xn_t[par][:], tmp[:], acc.B[:], ALU.add), inc=st)
        p.wait("scalar", st)
        p.dma("scalar", lambda e, ci=ci, b=xn_t[par]: e.dma_start(
            out=xn_out[ci * 128:(ci + 1) * 128, :], in_=b[:]))
        p.barrier()


def launch_proj_res_ln(h_fm, w_T, bias, x_fm, rms_pre_eps=None, skip_ln=False):
    K_in = h_fm.shape[0]
    p = Prog("prl")
    wdt = F32 if w_T.dtype == np.float32 else BF16
    h_in = p.dram_in("h", [K_in, TC], BF16)
    w_in = p.dram_in("w", [D // 128, 128, K_in // 128, 128], wdt)
    x_in = p.dram_in("x", [D, TC], F32)
    xp_out = p.dram_out("xp", [D, TC], F32)
    xn_out = None if skip_ln else p.dram_out("xn", [D, TC], BF16)
    y_dram = p.dram_tmp("y", [D, TC], F32)
    bt = None
    if bias is not None:
        b_in = p.dram_in("b", [128, 16], F32)
        bt = p.sbuf("bt", [128, 16], F32)
        p.dma("sync", lambda e: e.dma_start(out=bt[:], in_=b_in))
    yslice = p.sbuf("ysl", [128, 512], F32)
    racc = LNAcc(p, TC, tag="rms", want_s1=False) if rms_pre_eps is not None else None
    ev = p.sem("ev")
    p.barrier()

    def epilogue(mi, nb, ps):
        if bt is not None:
            p.op("vector", lambda v: v.tensor_scalar(
                yslice[:], ps[:], bt[:, mi:mi + 1], None, ALU.add), inc=ev)
        else:
            p.op("vector", lambda v: v.tensor_copy(yslice[:], ps[:]), inc=ev)
        if racc is not None:
            p.wait("scalar", ev)
            p.wait("tensor", ev)
            racc.slice_stats(yslice[:, :], nb, first=(mi == 0), last=(mi == 15))
        p.wait("sync", ev)
        p.dma("sync", lambda e, mi=mi, nb=nb: e.dma_start(
            out=y_dram[mi * 128:(mi + 1) * 128, nb * 512:(nb + 1) * 512],
            in_=yslice[:]))

    emit_proj_stream(p, h_in, w_in, K_in, D, TC, epilogue, tag="prj")
    p.barrier()
    A = None
    if racc is not None:
        racc.rms_finalize(D, rms_pre_eps)
        A = racc.A
    _emit_res_ln_write(p, y_dram, x_in, xp_out, xn_out, pre_scale_A=A)

    w_pk = pack_w(w_T)
    in_maps = []
    for c in range(NCORES):
        m = {"h": np.ascontiguousarray(h_fm[:, c * TC:(c + 1) * TC]),
             "w": w_pk,
             "x": np.ascontiguousarray(x_fm[:, c * TC:(c + 1) * TC])}
        if bias is not None:
            m["b"] = np.ascontiguousarray(
                bias.astype(np.float32).reshape(16, 128).T)
        in_maps.append(m)
    res = p.run(in_maps)
    xp = np.concatenate([r["xp"] for r in res], axis=1)
    xn = (None if skip_ln else np.concatenate([r["xn"] for r in res], axis=1))
    return xp, xn


def launch_ffn_up(xn_fm, wg_T, wu_T):
    I = wg_T.shape[1]
    assert I % 128 == 0
    p = Prog("ffnup")
    xn_in = p.dram_in("xn", [D, TC], BF16)
    wg_in = p.dram_in("wg", [I // 128, 128, D // 128, 128], BF16)
    wu_in = p.dram_in("wu", [I // 128, 128, D // 128, 128], BF16)
    h_out = p.dram_out("h", [I, TC], BF16)

    KC = D // 128
    MT = I // 128
    hb = p.sbuf("hb", [128, KC, 512], BF16)
    wb = [p.sbuf(f"wb{i}", [128, KC, 128], BF16) for i in range(4)]
    psg = [p.psum(f"psg{i}", [128, 512]) for i in range(2)]
    psu = [p.psum(f"psu{i}", [128, 512]) for i in range(2)]
    gs = p.sbuf("gs", [128, 512], F32)
    us = p.sbuf("us", [128, 512], F32)
    hs = p.sbuf("hs", [128, 512], BF16)
    mmd = p.sem("mmd")
    ev = p.sem("ev")
    xn4 = xn_in.rearrange("(kc pp) t -> pp kc t", pp=128)
    for nb in range(4):
        p.dma("sync", lambda e, nb=nb: e.dma_start(
            out=hb[:], in_=xn4[:, :, nb * 512:(nb + 1) * 512]))
        p.dma("gpsimd", lambda e: e.dma_start(out=wb[0][:], in_=wg_in[0]))
        p.dma("gpsimd", lambda e: e.dma_start(out=wb[2][:], in_=wu_in[0]))
        p.barrier()
        for mi in range(MT):
            par = mi % 2
            if mi + 1 < MT:
                p.dma("gpsimd", lambda e, mi=mi, b=wb[1 - par]: e.dma_start(
                    out=b[:], in_=wg_in[mi + 1]))
                p.dma("gpsimd", lambda e, mi=mi, b=wb[3 - par]: e.dma_start(
                    out=b[:], in_=wu_in[mi + 1]))
            for kc in range(KC):
                p.op("tensor", lambda t, b=wb[par], kc=kc, s=(kc == 0),
                     st=(kc == KC - 1), pp=psg[par]: t.matmul(
                         pp[:], b[:, kc, :], hb[:, kc, :], start=s, stop=st),
                     inc=(mmd if kc == KC - 1 else None))
            for kc in range(KC):
                p.op("tensor", lambda t, b=wb[2 + par], kc=kc, s=(kc == 0),
                     st=(kc == KC - 1), pp=psu[par]: t.matmul(
                         pp[:], b[:, kc, :], hb[:, kc, :], start=s, stop=st),
                     inc=(mmd if kc == KC - 1 else None))
            p.wait("scalar", mmd)
            p.op("scalar", lambda s, pp=psg[par]: s.activation(gs[:], pp[:], AF.Silu))
            p.op("scalar", lambda s, pp=psu[par]: s.activation(us[:], pp[:], AF.Identity),
                 inc=ev)
            p.wait("vector", ev)
            p.op("vector", lambda v: v.tensor_tensor(hs[:], gs[:], us[:], ALU.mult),
                 inc=ev)
            p.wait("sync", ev)
            p.dma("sync", lambda e, mi=mi, nb=nb: e.dma_start(
                out=h_out[mi * 128:(mi + 1) * 128, nb * 512:(nb + 1) * 512],
                in_=hs[:]))
            p.barrier()
    wg_pk, wu_pk = pack_w(wg_T), pack_w(wu_T)
    in_maps = [{"xn": np.ascontiguousarray(xn_fm[:, c * TC:(c + 1) * TC]),
                "wg": wg_pk, "wu": wu_pk} for c in range(NCORES)]
    res = p.run(in_maps)
    return np.concatenate([r["h"] for r in res], axis=1)


def launch_ghost_dequant(gate_idxT, up_idxT, down_idxT, lut):
    p = Prog("gdq")
    RG = D // NCORES
    RD = 5504 // NCORES
    g_in = p.dram_in("g", [RG, 5504], I32)
    u_in = p.dram_in("u", [RG, 5504], I32)
    d_in = p.dram_in("d", [RD, D], I32)
    lut_in = p.dram_in("lut", [1, 16], F32)
    g_out = p.dram_out("go", [RG, 5504], BF16)
    u_out = p.dram_out("uo", [RG, 5504], BF16)
    d_out = p.dram_out("do", [RD, D], BF16)
    lutb = p.sbuf("lutb", [128, 16], F32)
    p.dma("gpsimd", lambda e: e.dma_start(out=lutb[:], in_=lut_in.broadcast_to([128, 16])))
    p.barrier()

    idxt = p.sbuf("idxt", [128, 5504], I32)
    idxf = p.sbuf("idxf", [128, 5504], F32)
    eqm = p.sbuf("eqm", [128, 5504], F32)
    deq = p.sbuf("deq", [128, 5504], F32)
    deqb = p.sbuf("deqb", [128, 5504], BF16)

    def dq_rows(src, dst, nrows, ncols):
        nt = (nrows + 127) // 128
        for i in range(nt):
            r0 = i * 128
            r1 = min(nrows, r0 + 128)
            pr = r1 - r0
            cw = p.sem("cw")
            p.dma("sync", lambda e, r0=r0, pr=pr: e.dma_start(
                out=idxt[:pr, :ncols], in_=src[r0:r0 + pr, :]))
            p.barrier()
            p.op("vector", lambda v, pr=pr: v.tensor_copy(
                idxf[:pr, :ncols], idxt[:pr, :ncols]))
            p.op("vector", lambda v, pr=pr: v.tensor_scalar(
                deq[:pr, :ncols], idxf[:pr, :ncols], 0.0, lutb[:pr, 0:1],
                ALU.is_equal, ALU.mult))
            for j in range(1, 16):
                p.op("vector", lambda v, j=j, pr=pr: v.tensor_scalar(
                    eqm[:pr, :ncols], idxf[:pr, :ncols], float(j), lutb[:pr, j:j + 1],
                    ALU.is_equal, ALU.mult))
                p.op("vector", lambda v, pr=pr: v.tensor_tensor(
                    deq[:pr, :ncols], deq[:pr, :ncols], eqm[:pr, :ncols], ALU.add))
            p.op("vector", lambda v, pr=pr: v.tensor_copy(
                deqb[:pr, :ncols], deq[:pr, :ncols]), inc=cw)
            p.wait("sync", cw)
            p.dma("sync", lambda e, r0=r0, pr=pr: e.dma_start(
                out=dst[r0:r0 + pr, :], in_=deqb[:pr, :ncols]))
            p.barrier()

    dq_rows(g_in, g_out, RG, 5504)
    dq_rows(u_in, u_out, RG, 5504)
    dq_rows(d_in, d_out, RD, D)

    in_maps = []
    for c in range(NCORES):
        in_maps.append({
            "g": np.ascontiguousarray(gate_idxT[c * RG:(c + 1) * RG, :]),
            "u": np.ascontiguousarray(up_idxT[c * RG:(c + 1) * RG, :]),
            "d": np.ascontiguousarray(down_idxT[c * RD:(c + 1) * RD, :]),
            "lut": lut.reshape(1, 16).astype(np.float32)})
    res = p.run(in_maps)
    wg = np.concatenate([r["go"] for r in res], axis=0)
    wu = np.concatenate([r["uo"] for r in res], axis=0)
    wd = np.concatenate([r["do"] for r in res], axis=0)
    return wg, wu, wd



def launch_qkv(xn_fm, wq_T, wk_T, wv_T):
    p = Prog("qkv")
    xn_in = p.dram_in("xn", [D, TC], BF16)
    wq_in = p.dram_in("wq", [D // 128, 128, D // 128, 128], F32)
    wk_in = p.dram_in("wk", [D // 128, 128, D // 128, 128], F32)
    wv_in = p.dram_in("wv", [D, D], F32)
    q_out = p.dram_out("q", [D, TC], BF16)
    k_out = p.dram_out("k", [D, TC], BF16)
    v_out = p.dram_out("v", [TC, D], BF16)

    qs = p.sbuf("qs", [128, 512], BF16)
    ev = p.sem("ev")

    def mk_epilogue(out_dram):
        def epi(mi, nb, ps):
            p.op("vector", lambda v: v.tensor_copy(qs[:], ps[:]), inc=ev)
            p.wait("sync", ev)
            p.dma("sync", lambda e, mi=mi, nb=nb: e.dma_start(
                out=out_dram[mi * 128:(mi + 1) * 128, nb * 512:(nb + 1) * 512],
                in_=qs[:]))
        return epi

    emit_proj_stream(p, xn_in, wq_in, D, D, TC, mk_epilogue(q_out), tag="q")
    p.barrier()
    emit_proj_stream(p, xn_in, wk_in, D, D, TC, mk_epilogue(k_out), tag="k")
    p.barrier()
    KC = D // 128
    xb = p.sbuf("xb", [128, KC, 128], BF16)
    wvb = p.sbuf("wvb", [128, KC, D], BF16)
    psv = [p.psum(f"psv{i}", [128, 512]) for i in range(2)]
    vs = p.sbuf("vs", [128, 512], BF16)
    mmv = p.sem("mmv")
    evv = p.sem("evv")
    xn4 = xn_in.rearrange("(kc pp) t -> pp kc t", pp=128)
    wv4 = wv_in.rearrange("(kc pp) m -> pp kc m", pp=128)
    p.dma("gpsimd", lambda e: e.dma_start(out=wvb[:], in_=wv4))
    p.barrier()
    for mt in range(TC // 128):
        p.dma("sync", lambda e, mt=mt: e.dma_start(
            out=xb[:], in_=xn4[:, :, mt * 128:(mt + 1) * 128]))
        p.barrier()
        for nv in range(4):
            par = nv % 2
            for kc in range(KC):
                p.op("tensor", lambda t, kc=kc, nv=nv, par=par, s=(kc == 0),
                     st=(kc == KC - 1): t.matmul(
                         psv[par][:], xb[:, kc, :], wvb[:, kc, nv * 512:(nv + 1) * 512],
                         start=s, stop=st), inc=(mmv if kc == KC - 1 else None))
            p.wait("vector", mmv)
            p.op("vector", lambda v, par=par: v.tensor_copy(vs[:], psv[par][:]),
                 inc=evv)
            p.wait("sync", evv)
            p.dma("sync", lambda e, mt=mt, nv=nv: e.dma_start(
                out=v_out[mt * 128:(mt + 1) * 128, nv * 512:(nv + 1) * 512],
                in_=vs[:]))
            p.barrier()
    wq_pk, wk_pk = pack_w(wq_T), pack_w(wk_T)
    in_maps = [{"xn": np.ascontiguousarray(xn_fm[:, c * TC:(c + 1) * TC]),
                "wq": wq_pk, "wk": wk_pk, "wv": wv_T} for c in range(NCORES)]
    res = p.run(in_maps)
    q = np.concatenate([r["q"] for r in res], axis=1)
    k = np.concatenate([r["k"] for r in res], axis=1)
    v = np.concatenate([r["v"] for r in res], axis=0)
    return q, k, v



def launch_attention(q_fm, k_fm, v_tm, n_heads_core=8):
    HD = 128
    RH = n_heads_core * HD
    p = Prog("att")
    q_in = p.dram_in("q", [RH, SEQ], BF16)
    k_in = p.dram_in("k", [RH, SEQ], BF16)
    v_in = p.dram_in("v", [SEQ, RH], BF16)
    o_out = p.dram_out("o", [RH, SEQ], BF16)
    scr = p.dram_tmp("mscr", [2, 2], F32)

    qh = p.sbuf("qh", [128, SEQ], BF16)
    kh = p.sbuf("kh", [128, SEQ], BF16)
    vh = p.sbuf("vh", [128, SEQ // 128, 128], BF16)
    sq = p.sbuf("sqh", [128, 512], F32)
    r1 = p.sbuf("r1", [1, SEQ], F32)
    mrow = p.sbuf("mrow", [1, 1], F32)
    mrow2 = p.sbuf("mrow2", [1, 1], F32)
    mprod = p.sbuf("mprod", [1, 1], F32)
    msqrt = p.sbuf("msqrt", [1, 1], F32)
    mneg = p.sbuf("mneg", [1, 1], F32)
    negm = p.sbuf("negm", [128, 1], F32)
    onescol = p.sbuf("onescol", [128, 1], F32)
    onescolb = p.sbuf("onescolb", [128, 1], BF16)
    onesb = p.sbuf("onesb", [1, 128], BF16)
    pt = [p.sbuf(f"pt{i}", [128, 512], BF16) for i in range(4)]
    osb = p.sbuf("osb", [128, 512], F32)
    rsb = p.sbuf("rsb", [128, 512], F32)
    rrow = p.sbuf("rrow", [1, 512], F32)
    rrowb = p.sbuf("rrowb", [1, 512], BF16)
    obf = p.sbuf("obf", [128, 512], BF16)
    ps_st = [p.psum(f"ps_st{i}", [128, 512]) for i in range(4)]
    ps_o = p.psum("ps_o", [128, 512])
    ps_sum = p.psum("ps_sum", [1, 512])
    ps_sq = p.psum("ps_sq", [1, 512])
    ps_r = ps_st[0]
    p.op("gpsimd", lambda g: g.memset(onescol[:], 1.0))
    p.op("gpsimd", lambda g: g.memset(onescolb[:], 1.0))
    p.op("gpsimd", lambda g: g.memset(onesb[:], 1.0))
    SCALE = 1.0 / np.sqrt(128.0)

    mmq = p.sem("mmq")
    red = p.sem("red")
    mms = p.sem("mms")
    expd = p.sem("expd")
    pvd = p.sem("pvd")
    od = p.sem("od")

    def head_max_norm(src_tile, out_row):
        for nb in range(SEQ // 512):
            sl = slice(nb * 512, (nb + 1) * 512)
            p.wait("scalar", mmq)
            p.op("scalar", lambda s, sl=sl: s.activation(sq[:], src_tile[:, sl],
                                                         AF.Square), inc=red)
            p.wait("tensor", red)
            p.op("tensor", lambda t, sl=sl: t.matmul(
                ps_sq[:], onescol[:], sq[:], start=True, stop=True), inc=mmq)
            p.wait("vector", mmq)
            p.op("vector", lambda v, sl=sl: v.tensor_copy(r1[:, sl], ps_sq[:]),
                 inc=red)
        p.wait("vector", red)
        p.op("vector", lambda v: v.reduce_max(out_row[:], r1[:], axis=AX.X), inc=red)
        p.wait("vector", red)

    for h in range(n_heads_core):
        p.dma("sync", lambda e, h=h: e.dma_start(
            out=qh[:], in_=q_in[h * 128:(h + 1) * 128, :]))
        p.dma("sync", lambda e, h=h: e.dma_start(
            out=kh[:], in_=k_in[h * 128:(h + 1) * 128, :]))
        p.dma("gpsimd", lambda e, h=h: e.dma_start(
            out=vh[:], in_=v_in.rearrange("(tc pp) d -> pp tc d", pp=128)
            [:, :, h * 128:(h + 1) * 128]))
        p.barrier()
        head_max_norm(qh, mrow)
        head_max_norm(kh, mrow2)
        fin = p.sem("fin")
        p.op("vector", lambda v: v.tensor_tensor(mprod[:], mrow[:], mrow2[:], ALU.mult),
             inc=fin)
        p.wait("scalar", fin)
        p.op("scalar", lambda s: s.activation(msqrt[:], mprod[:], AF.Sqrt), inc=fin)
        p.wait("vector", fin)
        p.op("vector", lambda v: v.tensor_scalar(mneg[:], msqrt[:], -SCALE, None,
                                                 ALU.mult), inc=fin)
        p.barrier()
        p.dma("sync", lambda e: e.dma_start(out=scr[0:1, 0:1], in_=mneg[:1, :1]))
        p.barrier()
        p.dma("sync", lambda e: e.dma_start(
            out=negm[:], in_=scr[0:1, 0:1].broadcast_to([128, 1])))
        p.barrier()
        for tqb in range(SEQ // 512):
            qsl = slice(tqb * 512, (tqb + 1) * 512)
            NKC = SEQ // 128
            for tkc in range(NKC):
                par = tkc % 4
                if tkc >= 4:
                    p.wait("tensor", expd, val=expd.n - 3)
                p.op("tensor", lambda t, tkc=tkc, par=par, qsl=qsl: t.matmul(
                    ps_st[par][:], kh[:, tkc * 128:(tkc + 1) * 128], qh[:, qsl],
                    start=True, stop=True), inc=mms)
                p.wait("scalar", mms)
                if tkc >= 4:
                    p.wait("scalar", pvd, val=pvd.n - 3)
                p.op("scalar", lambda s, par=par: s.activation(
                    pt[par][:], ps_st[par][:], AF.Exp, bias=negm[:], scale=SCALE),
                    inc=expd)
                p.wait("tensor", expd)
                p.op("tensor", lambda t, tkc=tkc, par=par, h=h: t.matmul(
                    ps_o[:], vh[:, tkc, :], pt[par][:],
                    start=(tkc == 0), stop=(tkc == NKC - 1)))
                p.op("tensor", lambda t, tkc=tkc, par=par: t.matmul(
                    ps_sum[:], onescolb[:], pt[par][:],
                    start=(tkc == 0), stop=(tkc == NKC - 1)), inc=pvd)
            p.wait("vector", pvd)
            p.op("vector", lambda v: v.reciprocal(rrow[:], ps_sum[:]))
            p.op("vector", lambda v: v.tensor_copy(rrowb[:], rrow[:]), inc=od)
            p.wait("tensor", od)
            p.op("tensor", lambda t: t.matmul(ps_r[:], onesb[:1, :],
                                              rrowb[:1, :], start=True, stop=True),
                 inc=od)
            p.wait("vector", od)
            p.op("vector", lambda v: v.tensor_copy(osb[:], ps_o[:]))
            p.op("vector", lambda v: v.tensor_copy(rsb[:], ps_r[:]))
            p.op("vector", lambda v: v.tensor_tensor(obf[:], osb[:], rsb[:], ALU.mult),
                 inc=od)
            p.wait("sync", od)
            p.dma("sync", lambda e, h=h, tqb=tqb: e.dma_start(
                out=o_out[h * 128:(h + 1) * 128, tqb * 512:(tqb + 1) * 512],
                in_=obf[:]))
            p.barrier()

    in_maps = []
    for c in range(NCORES):
        b, hg = c // 2, c % 2
        rows = slice(hg * RH, (hg + 1) * RH)
        cols = slice(b * SEQ, (b + 1) * SEQ)
        in_maps.append({
            "q": np.ascontiguousarray(q_fm[rows, cols]),
            "k": np.ascontiguousarray(k_fm[rows, cols]),
            "v": np.ascontiguousarray(v_tm[cols, rows])})
    res = p.run(in_maps)
    o_fm = np.zeros((D, TOK), dtype=bfdt)
    for c in range(NCORES):
        b, hg = c // 2, c % 2
        o_fm[hg * RH:(hg + 1) * RH, b * SEQ:(b + 1) * SEQ] = res[c]["o"]
    return o_fm



def launch_lin_attn(xn_fm, w_qkv):
    p = Prog("lina")
    RH = 1024
    xn_in = p.dram_in("xn", [D, SEQ], BF16)
    wq_in = p.dram_in("wq", [D, RH], F32)
    wk_in = p.dram_in("wk", [D, RH], F32)
    wv_in = p.dram_in("wv", [D, RH], F32)
    bm_in = p.dram_in("bm", [4, 128], BF16)
    o_out = p.dram_out("o", [RH, SEQ], BF16)
    vkdram = p.dram_tmp("vkd", [32, 1024], BF16)
    ksdram = p.dram_tmp("ksd", [8, 128], F32)

    KC = D // 128
    wq = p.sbuf("wq", [128, KC, RH], BF16)
    wk = p.sbuf("wk", [128, KC, RH], BF16)
    wv = p.sbuf("wv", [128, KC, RH], BF16)
    bm = p.sbuf("bm", [4, 128], BF16)
    xnb = p.sbuf("xnb", [128, KC, 512], BF16)
    kb = p.sbuf("kb", [128, 4, RH], BF16)
    vb = p.sbuf("vb", [128, 4, RH], BF16)
    vk_sb = p.sbuf("vk_sb", [32, 1024], BF16)
    ks_sb = p.sbuf("ks_sb", [1, 1024], F32)
    bd = [p.sbuf(f"bd{g}", [128, 128], BF16) for g in range(8)]
    ksb = [p.sbuf(f"ksb{g}", [128, 4], BF16) for g in range(8)]
    qg = p.sbuf("qg", [128, 512], BF16)
    den = p.sbuf("den", [4, 512], F32)
    denb = p.sbuf("denb", [4, 512], BF16)
    rb = p.sbuf("rb", [128, 512], F32)
    qt = p.sbuf("qt", [128, 512], BF16)
    ob = p.sbuf("ob", [128, 512], BF16)
    ps_kv = [p.psum(f"ps_kv{i}", [128, 512]) for i in range(2)]
    ps_vk = p.psum("ps_vk", [32, 1024])
    ps_ks = p.psum("ps_ks", [1, 1024])
    ps_d = p.psum("ps_d", [4, 512])
    ps_r = p.psum("ps_r", [128, 512])
    ps_q = ps_kv[0]
    ps_of = ps_kv[1]
    onescolb = p.sbuf("onescolb", [128, 1], BF16)
    p.op("gpsimd", lambda g: g.memset(onescolb[:], 1.0))

    mmd = p.sem("mmd")
    ev = p.sem("ev")
    xn4 = xn_in.rearrange("(kc pp) t -> pp kc t", pp=128)
    p.dma("gpsimd", lambda e: e.dma_start(
        out=wq[:], in_=wq_in.rearrange("(kc pp) m -> pp kc m", pp=128)))
    p.dma("gpsimd", lambda e: e.dma_start(
        out=wk[:], in_=wk_in.rearrange("(kc pp) m -> pp kc m", pp=128)))
    p.dma("gpsimd", lambda e: e.dma_start(
        out=wv[:], in_=wv_in.rearrange("(kc pp) m -> pp kc m", pp=128)))
    p.dma("sync", lambda e: e.dma_start(out=bm[:], in_=bm_in))
    p.barrier()

    NTB = SEQ // 512
    for tb in range(NTB):
        p.dma("sync", lambda e, tb=tb: e.dma_start(
            out=xnb[:], in_=xn4[:, :, tb * 512:(tb + 1) * 512]))
        p.barrier()
        for tt in range(4):
            for src_w, dst, do_relu in ((wk, kb, True), (wv, vb, False)):
                for ms in range(2):
                    par = ms % 2
                    msl = slice(ms * 512, (ms + 1) * 512)
                    for kc in range(KC):
                        p.op("tensor", lambda t, kc=kc, tt=tt, msl=msl, par=par,
                             w=src_w, s=(kc == 0), st=(kc == KC - 1): t.matmul(
                                 ps_kv[par][:], xnb[:, kc, tt * 128:(tt + 1) * 128],
                                 w[:, kc, msl], start=s, stop=st),
                             inc=(mmd if kc == KC - 1 else None))
                    p.wait("scalar", mmd)
                    if do_relu:
                        p.op("scalar", lambda s, par=par, tt=tt, msl=msl, d=dst:
                             s.activation(d[:, tt, msl], ps_kv[par][:], AF.Relu),
                             inc=ev)
                    else:
                        p.op("scalar", lambda s, par=par, tt=tt, msl=msl, d=dst:
                             s.activation(d[:, tt, msl], ps_kv[par][:], AF.Identity),
                             inc=ev)
                    p.wait("tensor", ev)
            last = (tb == NTB - 1 and tt == 3)
            for h in range(32):
                hsl = slice(h * 32, (h + 1) * 32)
                first = (tb == 0 and tt == 0 and h % 16 == 0)
                p.op("tensor", lambda t, tt=tt, hsl=hsl, f=first, l=last: t.matmul(
                    ps_vk[:, hsl], kb[:, tt, hsl], vb[:, tt, hsl],
                    start=f, stop=l))
                p.op("tensor", lambda t, tt=tt, hsl=hsl, f=first, l=last: t.matmul(
                    ps_ks[:, hsl], onescolb[:], kb[:, tt, hsl],
                    start=f, stop=l), inc=mmd)
        p.barrier()
    p.op("vector", lambda v: v.tensor_copy(vk_sb[:], ps_vk[:]))
    p.op("vector", lambda v: v.tensor_copy(ks_sb[:], ps_ks[:]), inc=ev)
    p.wait("sync", ev)
    p.dma("sync", lambda e: e.dma_start(out=vkdram, in_=vk_sb[:]))
    p.dma("sync", lambda e: e.dma_start(
        out=ksdram.rearrange("a b -> (a b)").rearrange("(x y) -> x y", x=1),
        in_=ks_sb[:]))
    for g in range(8):
        p.op("gpsimd", lambda gg, g=g: gg.memset(bd[g][:], 0.0))
        p.op("gpsimd", lambda gg, g=g: gg.memset(ksb[g][:], 0.0))
    p.barrier()
    for g in range(8):
        for j in range(4):
            h = 4 * g + j
            p.dma("sync", lambda e, g=g, j=j, h=h: e.dma_start(
                out=bd[g][32 * j:32 * j + 32, 32 * j:32 * j + 32],
                in_=vkdram[0:32, h * 32:(h + 1) * 32]))
            p.dma("gpsimd", lambda e, g=g, j=j, h=h: e.dma_start(
                out=ksb[g][32 * j:32 * j + 32, j:j + 1],
                in_=ksdram.rearrange("a b -> (a b)").rearrange(
                    "(x y) -> x y", y=1)[h * 32:(h + 1) * 32, :]))
    p.barrier()
    for tb in range(NTB):
        p.dma("sync", lambda e, tb=tb: e.dma_start(
            out=xnb[:], in_=xn4[:, :, tb * 512:(tb + 1) * 512]))
        p.barrier()
        for g in range(8):
            gsl = slice(g * 128, (g + 1) * 128)
            for kc in range(KC):
                p.op("tensor", lambda t, kc=kc, gsl=gsl, s=(kc == 0),
                     st=(kc == KC - 1): t.matmul(
                         ps_q[:], wq[:, kc, gsl], xnb[:, kc, :], start=s, stop=st),
                     inc=(mmd if kc == KC - 1 else None))
            p.wait("scalar", mmd)
            p.op("scalar", lambda s: s.activation(qg[:], ps_q[:], AF.Relu), inc=ev)
            p.wait("tensor", ev)
            p.op("tensor", lambda t, g=g: t.matmul(
                ps_d[:], ksb[g][:], qg[:], start=True, stop=True), inc=mmd)
            p.wait("vector", mmd)
            p.op("vector", lambda v: v.tensor_scalar(den[:], ps_d[:], 1e-15, None,
                                                     ALU.add), inc=ev)
            p.wait("vector", ev)
            p.op("vector", lambda v: v.reciprocal(den[:], den[:]), inc=ev)
            p.wait("vector", ev)
            p.op("vector", lambda v: v.tensor_copy(denb[:], den[:]), inc=ev)
            p.wait("tensor", ev)
            p.op("tensor", lambda t: t.matmul(ps_r[:], bm[:], denb[:],
                                              start=True, stop=True), inc=mmd)
            p.wait("vector", mmd)
            p.op("vector", lambda v: v.tensor_copy(rb[:], ps_r[:]), inc=ev)
            p.wait("vector", ev)
            p.op("vector", lambda v: v.tensor_tensor(qt[:], qg[:], rb[:], ALU.mult),
                 inc=ev)
            p.wait("tensor", ev)
            p.op("tensor", lambda t, g=g: t.matmul(
                ps_of[:], bd[g][:], qt[:], start=True, stop=True), inc=mmd)
            p.wait("vector", mmd)
            p.op("vector", lambda v: v.tensor_copy(ob[:], ps_of[:]), inc=ev)
            p.wait("sync", ev)
            p.dma("sync", lambda e, g=g, tb=tb: e.dma_start(
                out=o_out[g * 128:(g + 1) * 128, tb * 512:(tb + 1) * 512],
                in_=ob[:]))
            p.barrier()

    bm_host = np.zeros((4, 128), dtype=bfdt)
    for j in range(4):
        bm_host[j, 32 * j:32 * j + 32] = 1.0
    in_maps = []
    for c in range(NCORES):
        b, hg = c // 2, c % 2
        cols = slice(b * SEQ, (b + 1) * SEQ)
        wq_s = np.ascontiguousarray(w_qkv[hg * RH:(hg + 1) * RH, :].T)
        wk_s = np.ascontiguousarray(w_qkv[D + hg * RH:D + (hg + 1) * RH, :].T)
        wv_s = np.ascontiguousarray(w_qkv[2 * D + hg * RH:2 * D + (hg + 1) * RH, :].T)
        in_maps.append({"xn": np.ascontiguousarray(xn_fm[:, cols]),
                        "wq": wq_s, "wk": wk_s, "wv": wv_s, "bm": bm_host})
    res = p.run(in_maps)
    o_fm = np.zeros((D, TOK), dtype=bfdt)
    for c in range(NCORES):
        b, hg = c // 2, c % 2
        o_fm[hg * RH:(hg + 1) * RH, b * SEQ:(b + 1) * SEQ] = res[c]["o"]
    return o_fm



def launch_fp4_dequant(gu_u8, gu_scalesT, dn_u8, dn_scalesT):
    RG = 11264 // NCORES
    RD = 2048 // NCORES
    p = Prog("fp4")
    g_in = p.dram_in("g", [RG, 1024], U8)
    gs_in = p.dram_in("gs", [RG, 16], F16)
    d_in = p.dram_in("d", [RD, 2816], U8)
    ds_in = p.dram_in("ds", [RD, 44], F16)
    g_out = p.dram_out("go", [RG, 2048], BF16)
    d_out = p.dram_out("do", [RD, 5632], BF16)

    W = 2816
    ut = p.sbuf("ut", [128, 1408], U8)
    ui = p.sbuf("ui", [128, 1408], I32)
    nib = p.sbuf("nib", [128, W], I32)
    nf = p.sbuf("nf", [128, W], F32)
    m8i = p.sbuf("m8i", [128, W], I32)
    m8 = p.sbuf("m8", [128, W], F32)
    b0i = p.sbuf("b0i", [128, W], I32)
    b0 = p.sbuf("b0", [128, W], F32)
    e_t = p.sbuf("e_t", [128, W], F32)
    t1 = p.sbuf("t1", [128, W], F32)
    t2 = p.sbuf("t2", [128, W], F32)
    sgn = p.sbuf("sgn", [128, W], F32)
    msk = p.sbuf("msk", [128, W], F32)
    aa = p.sbuf("aa", [128, W], F32)
    bb2 = p.sbuf("bb2", [128, W], F32)
    val = p.sbuf("val", [128, W], F32)
    scf = p.sbuf("scf", [128, 22], F32)
    sce = p.sbuf("sce", [128, W], F32)
    ones128 = p.sbuf("ones128", [128, 128], F32)
    outb = p.sbuf("outb", [128, W], BF16)
    p.op("gpsimd", lambda g: g.memset(ones128[:], 1.0))

    def dq(src, ssrc, dst, nrows, nwords, ngroups):
        nt = (nrows + 127) // 128
        done = p.sem("dq")
        WB = 1408
        for i in range(nt):
          r0 = i * 128
          pr = min(nrows, r0 + 128) - r0
          for wb0 in range(0, nwords, WB):
            wcnt = min(WB, nwords - wb0)
            ncols = wcnt * 2
            g0 = (wb0 * 2) // 128
            ng = ncols // 128
            p.dma("sync", lambda e, r0=r0, pr=pr, wb0=wb0, wcnt=wcnt: e.dma_start(
                out=ut[:pr, :wcnt], in_=src[r0:r0 + pr, wb0:wb0 + wcnt]))
            p.dma("gpsimd", lambda e, r0=r0, pr=pr, g0=g0, ng=ng: e.dma_start(
                out=scf[:pr, :ng], in_=ssrc[r0:r0 + pr, g0:g0 + ng]))
            p.barrier()

            def V(fn):
                p.op("vector", fn)
            V(lambda v: v.tensor_copy(ui[:pr, :wcnt], ut[:pr, :wcnt]))
            V(lambda v: v.tensor_scalar(nib[:pr, 0:ncols:2], ui[:pr, :wcnt],
                                        15, None, ALU.bitwise_and))
            V(lambda v: v.tensor_scalar(nib[:pr, 1:ncols:2], ui[:pr, :wcnt],
                                        4, 15, ALU.logical_shift_right,
                                        ALU.bitwise_and))
            V(lambda v: v.tensor_copy(nf[:pr, :ncols], nib[:pr, :ncols]))
            V(lambda v: v.tensor_scalar(sgn[:pr, :ncols], nf[:pr, :ncols],
                                        8.0, -2.0, ALU.is_ge, ALU.mult))
            V(lambda v: v.tensor_scalar(sgn[:pr, :ncols], sgn[:pr, :ncols],
                                        1.0, None, ALU.add))
            V(lambda v: v.tensor_scalar(m8i[:pr, :ncols], nib[:pr, :ncols],
                                        7, None, ALU.bitwise_and))
            V(lambda v: v.tensor_copy(m8[:pr, :ncols], m8i[:pr, :ncols]))
            V(lambda v: v.tensor_scalar(b0i[:pr, :ncols], nib[:pr, :ncols],
                                        1, None, ALU.bitwise_and))
            V(lambda v: v.tensor_copy(b0[:pr, :ncols], b0i[:pr, :ncols]))
            V(lambda v: v.tensor_tensor(e_t[:pr, :ncols], m8[:pr, :ncols],
                                        b0[:pr, :ncols], ALU.subtract))
            V(lambda v: v.tensor_scalar(e_t[:pr, :ncols], e_t[:pr, :ncols],
                                        0.5, None, ALU.mult))
            V(lambda v: v.tensor_tensor(t1[:pr, :ncols], e_t[:pr, :ncols],
                                        e_t[:pr, :ncols], ALU.mult))
            V(lambda v: v.tensor_scalar(t1[:pr, :ncols], t1[:pr, :ncols],
                                        5.0, None, ALU.add))
            V(lambda v: v.tensor_tensor(t2[:pr, :ncols], t1[:pr, :ncols],
                                        e_t[:pr, :ncols], ALU.mult))
            V(lambda v: v.tensor_scalar(t2[:pr, :ncols], t2[:pr, :ncols],
                                        6.0, 1.0 / 12.0, ALU.add, ALU.mult))
            V(lambda v: v.tensor_scalar(aa[:pr, :ncols], b0[:pr, :ncols],
                                        0.5, 1.0, ALU.mult, ALU.add))
            V(lambda v: v.tensor_tensor(aa[:pr, :ncols], aa[:pr, :ncols],
                                        t2[:pr, :ncols], ALU.mult))
            V(lambda v: v.tensor_scalar(bb2[:pr, :ncols], b0[:pr, :ncols],
                                        0.5, None, ALU.mult))
            V(lambda v: v.tensor_scalar(msk[:pr, :ncols], m8[:pr, :ncols],
                                        2.0, None, ALU.is_ge))
            V(lambda v: v.tensor_tensor(aa[:pr, :ncols], aa[:pr, :ncols],
                                        bb2[:pr, :ncols], ALU.subtract))
            V(lambda v: v.tensor_tensor(val[:pr, :ncols], msk[:pr, :ncols],
                                        aa[:pr, :ncols], ALU.mult))
            V(lambda v: v.tensor_tensor(val[:pr, :ncols], val[:pr, :ncols],
                                        bb2[:pr, :ncols], ALU.add))
            V(lambda v: v.tensor_tensor(val[:pr, :ncols], val[:pr, :ncols],
                                        sgn[:pr, :ncols], ALU.mult))
            for gi in range(ng):
                V(lambda v, gi=gi: v.tensor_single_scalar(
                    sce[:pr, gi * 128:(gi + 1) * 128], ones128[:pr, :],
                    scf[:pr, gi:gi + 1], ALU.mult))
            p.op("vector", lambda v: v.tensor_tensor(
                outb[:pr, :ncols], val[:pr, :ncols], sce[:pr, :ncols], ALU.mult),
                inc=done)
            p.wait("sync", done)
            p.dma("sync", lambda e, r0=r0, pr=pr, wb0=wb0, ncols=ncols: e.dma_start(
                out=dst[r0:r0 + pr, 2 * wb0:2 * wb0 + ncols], in_=outb[:pr, :ncols]))
            p.barrier()

    dq(g_in, gs_in, g_out, RG, 1024, 16)
    dq(d_in, ds_in, d_out, RD, 2816, 44)

    in_maps = []
    for c in range(NCORES):
        in_maps.append({
            "g": np.ascontiguousarray(gu_u8[c * RG:(c + 1) * RG, :]),
            "gs": np.ascontiguousarray(gu_scalesT[c * RG:(c + 1) * RG, :]),
            "d": np.ascontiguousarray(dn_u8[c * RD:(c + 1) * RD, :]),
            "ds": np.ascontiguousarray(dn_scalesT[c * RD:(c + 1) * RD, :])})
    res = p.run(in_maps)
    w_gu = np.concatenate([r["go"] for r in res], axis=0)
    w_dn = np.concatenate([r["do"] for r in res], axis=0)
    return w_gu, w_dn



def launch_mamba_proj(xn_fm, in_proj_w, dt_bias):
    p = Prog("mproj")
    KC = D // 128
    xn_in = p.dram_in("xn", [D, TC], BF16)
    w_in = p.dram_in("w", [66, 128, D // 128, 128], F32)
    wl_in = p.dram_in("wl", [128, D // 128, 64], F32)
    db_in = p.dram_in("db", [64, 1], F32)
    gate_out = p.dram_out("gate", [4096, TC], BF16)
    hbc_out = p.dram_out("hbc", [4352, TC], F32)
    dt_out = p.dram_out("dt", [64, TC], F32)

    hb = p.sbuf("hb", [128, KC, 512], BF16)
    wb = [p.sbuf(f"wb{i}", [128, KC, 128], BF16) for i in range(2)]
    ps = [p.psum(f"ps{i}", [128, 512]) for i in range(2)]
    gb = p.sbuf("gb", [128, 512], BF16)
    fb = p.sbuf("fb", [128, 512], F32)
    dtb = p.sbuf("dtb", [64, 1], F32)
    t_r = p.sbuf("t_r", [64, 512], F32)
    t_a = p.sbuf("t_a", [64, 512], F32)
    t_e = p.sbuf("t_e", [64, 512], F32)
    t_l = p.sbuf("t_l", [64, 512], F32)
    t_b = p.sbuf("t_b", [64, 512], F32)
    mmd = p.sem("mmd")
    ev = p.sem("ev")
    xn4 = xn_in.rearrange("(kc pp) t -> pp kc t", pp=128)
    p.dma("sync", lambda e: e.dma_start(out=dtb[:], in_=db_in))
    p.barrier()
    m_tiles = [(mi * 128, 128) for mi in range(66)] + [(8448, 64)]
    for nb in range(4):
        p.dma("sync", lambda e, nb=nb: e.dma_start(
            out=hb[:], in_=xn4[:, :, nb * 512:(nb + 1) * 512]))
        p.dma("gpsimd", lambda e: e.dma_start(out=wb[0][:, :, 0:128],
                                              in_=w_in[0]))
        p.barrier()
        for mi, (m0, msz) in enumerate(m_tiles):
            par = mi % 2
            if mi + 1 < len(m_tiles):
                n0, nsz = m_tiles[mi + 1]
                if nsz == 128:
                    p.dma("gpsimd", lambda e, mi=mi, b=wb[1 - par]: e.dma_start(
                        out=b[:], in_=w_in[mi + 1]))
                else:
                    p.dma("gpsimd", lambda e, b=wb[1 - par], nsz=nsz: e.dma_start(
                        out=b[:, :, 0:nsz], in_=wl_in))
            for kc in range(KC):
                p.op("tensor", lambda t, b=wb[par], kc=kc, s=(kc == 0),
                     st=(kc == KC - 1), pp=ps[par], msz=msz: t.matmul(
                         pp[:msz, :], b[:, kc, 0:msz], hb[:, kc, :],
                         start=s, stop=st), inc=(mmd if kc == KC - 1 else None))
            tsl = slice(nb * 512, (nb + 1) * 512)
            if m0 < 4096:
                p.wait("vector", mmd)
                p.op("vector", lambda v, par=par: v.tensor_copy(gb[:], ps[par][:]),
                     inc=ev)
                p.wait("sync", ev)
                p.dma("sync", lambda e, m0=m0, tsl=tsl: e.dma_start(
                    out=gate_out[m0:m0 + 128, tsl], in_=gb[:]))
            elif m0 < 8448:
                p.wait("vector", mmd)
                p.op("vector", lambda v, par=par: v.tensor_copy(fb[:], ps[par][:]),
                     inc=ev)
                p.wait("sync", ev)
                p.dma("sync", lambda e, m0=m0, tsl=tsl: e.dma_start(
                    out=hbc_out[m0 - 4096:m0 - 4096 + 128, tsl], in_=fb[:]))
            else:
                p.wait("vector", mmd)
                p.op("vector", lambda v, par=par: v.tensor_scalar(
                    t_b[:], ps[par][:64, :], dtb[:], None, ALU.add))
                p.op("vector", lambda v: v.tensor_scalar(
                    t_r[:], t_b[:], 0.0, None, ALU.max), inc=ev)
                p.wait("scalar", ev)
                p.op("scalar", lambda s: s.activation(t_a[:], t_b[:], AF.Abs))
                p.op("scalar", lambda s: s.activation(t_e[:], t_a[:], AF.Exp,
                                                      scale=-1.0))
                p.op("scalar", lambda s: s.activation(t_l[:], t_e[:], AF.Ln,
                                                      bias=1.0), inc=ev)
                p.wait("vector", ev)
                p.op("vector", lambda v: v.tensor_tensor(t_b[:], t_r[:], t_l[:],
                                                         ALU.add), inc=ev)
                p.wait("sync", ev)
                p.dma("sync", lambda e, tsl=tsl: e.dma_start(
                    out=dt_out[:, tsl], in_=t_b[:]))
            p.barrier()
    w_pk = pack_w(np.ascontiguousarray(in_proj_w[:, :66 * 128]))
    wl_pk = np.ascontiguousarray(
        in_proj_w[:, 66 * 128:].reshape(16, 128, 64).transpose(1, 0, 2))
    in_maps = [{"xn": np.ascontiguousarray(xn_fm[:, c * TC:(c + 1) * TC]),
                "w": w_pk, "wl": wl_pk,
                "db": dt_bias.reshape(64, 1).astype(np.float32)}
               for c in range(NCORES)]
    res = p.run(in_maps)
    gate = np.concatenate([r["gate"] for r in res], axis=1)
    hbc = np.concatenate([r["hbc"] for r in res], axis=1)
    dt = np.concatenate([r["dt"] for r in res], axis=1)
    return gate, hbc, dt



def launch_conv(hbc_halo, conv_w, conv_b):
    p = Prog("conv")
    CH = 4352
    h_in = p.dram_in("h", [CH, TC + 3], F32)
    w_in = p.dram_in("w", [CH, 4], F32)
    b_in = p.dram_in("b", [CH, 1], F32)
    o_out = p.dram_out("o", [CH, TC], F32)
    xt = p.sbuf("xt", [128, TC + 3], F32)
    wt = p.sbuf("wt", [128, 4], F32)
    bt = p.sbuf("bt", [128, 1], F32)
    acc = p.sbuf("acc", [128, TC], F32)
    ob = p.sbuf("ob", [128, TC], F32)
    ev = p.sem("ev")
    for ci in range(CH // 128):
        r = slice(ci * 128, (ci + 1) * 128)
        p.dma("sync", lambda e, r=r: e.dma_start(out=xt[:], in_=h_in[r, :]))
        p.dma("scalar", lambda e, r=r: e.dma_start(out=wt[:], in_=w_in[r, :]))
        p.dma("scalar", lambda e, r=r: e.dma_start(out=bt[:], in_=b_in[r, :]))
        p.barrier()
        p.op("vector", lambda v: v.tensor_scalar(
            acc[:], xt[:, 0:TC], wt[:, 0:1], None, ALU.mult))
        for k in range(1, 4):
            p.op("vector", lambda v, k=k: v.scalar_tensor_tensor(
                acc[:], xt[:, k:k + TC], wt[:, k:k + 1], acc[:],
                ALU.mult, ALU.add), inc=(ev if k == 3 else None))
        p.wait("scalar", ev)
        p.op("scalar", lambda s: s.activation(ob[:], acc[:], AF.Silu, bias=bt[:]),
             inc=ev)
        p.wait("sync", ev)
        p.dma("sync", lambda e, r=r: e.dma_start(out=o_out[r, :], in_=ob[:]))
        p.barrier()
    in_maps = []
    for c in range(NCORES):
        t0 = c * TC
        if c % 2 == 0:
            halo = np.zeros((CH, 3), np.float32)
        else:
            halo = hbc_halo[:, t0 - 3:t0]
        m = {"h": np.ascontiguousarray(np.concatenate([halo, hbc_halo[:, t0:t0 + TC]], axis=1)),
             "w": conv_w.astype(np.float32),
             "b": conv_b.reshape(CH, 1).astype(np.float32)}
        in_maps.append(m)
    res = p.run(in_maps)
    return np.concatenate([r["o"] for r in res], axis=1)



def launch_ssm(x_tm_host, B_fm_host, C_fm_host, B_tm_host, dt_host, A_log, D_param):
    p = Prog("ssm")
    NH = 32
    x_in = p.dram_in("x", [SEQ, NH * 64], F32)
    bf_in = p.dram_in("bf", [128, SEQ], F32)
    cf_in = p.dram_in("cf", [128, SEQ], F32)
    bt_in = p.dram_in("bt", [SEQ, 128], F32)
    dt_in = p.dram_in("dt", [NH, SEQ], F32)
    al_in = p.dram_in("al", [NH, 1], F32)
    dp_in = p.dram_in("dp", [1, NH], F32)
    y_out = p.dram_out("y", [SEQ, NH * 64], BF16)
    cumdram = p.dram_tmp("cumd", [NH, SEQ], F32)
    sdodram = p.dram_tmp("sdod", [NH, SEQ], F32)
    cddram = p.dram_tmp("cdd", [NH, 16], F32)

    Bf = p.sbuf("Bf", [128, SEQ], F32)
    Cf = p.sbuf("Cf", [128, SEQ], F32)
    Bfb = p.sbuf("Bfb", [128, SEQ], BF16)
    Cfb = p.sbuf("Cfb", [128, SEQ], BF16)
    dt_r = p.sbuf("dt_r", [NH, SEQ], F32)
    dA = p.sbuf("dA", [NH, SEQ], F32)
    cum = p.sbuf("cum", [NH, SEQ], F32)
    wrow = p.sbuf("wrow", [NH, SEQ], F32)
    sdor = p.sbuf("sdor", [NH, SEQ], F32)
    cdr = p.sbuf("cdr", [NH, 16], F32)
    acol = p.sbuf("acol", [NH, 1], F32)
    tmpr = p.sbuf("tmpr", [NH, 256], F32)
    ident = p.sbuf("ident", [128, 128], F32)
    cumT = p.sbuf("cumT", [128, 32, NH], F32)
    dtT = p.sbuf("dtT", [128, 32, NH], F32)
    wT = p.sbuf("wT", [128, 32, NH], F32)
    trsb = p.sbuf("trsb", [128, NH], F32)
    dcol = p.sbuf("dcol", [128, NH], F32)
    cdb = p.sbuf("cdb", [128, NH * 16], F32)
    xb2 = p.sbuf("xb2", [128, 2, 64], BF16)
    bt2 = p.sbuf("bt2", [128, 2, 128], BF16)
    bwb = p.sbuf("bwb", [128, 2, 128], BF16)
    cbt = p.sbuf("cbt", [128, 2, 256], BF16)
    m1 = p.sbuf("m1", [128, 256], F32)
    sdob = p.sbuf("sdob", [128, 256], F32)
    cwb = p.sbuf("cwb", [128, 256], BF16)
    diff = p.sbuf("diff", [128, 256], F32)
    pte = p.sbuf("pte", [128, 256], F32)
    stb = [p.sbuf(f"stb{j}", [128, 256], BF16) for j in range(2)]
    maskC = [p.sbuf(f"maskC{j}", [128, 256], F32) for j in range(2)]
    zz = p.sbuf("zz", [128, 256], F32)
    Ssb = p.sbuf("Ssb", [128, NH, 2, 64], F32)
    Sbf = p.sbuf("Sbf", [128, 64], BF16)
    yhb = p.sbuf("yhb", [128, 64], BF16)
    ps_tr = p.psum("ps_tr", [128, NH])
    ps_cbt = p.psum("ps_cbt", [128, 256])
    ps_y = [p.psum(f"ps_y{i}", [128, 64]) for i in range(2)]
    ps_st = p.psum("ps_st", [128, 64])

    mmd = p.sem("mmd")
    ev = p.sem("ev")
    evd = p.sem("evD")
    scn = p.sem("scn")

    p.op("gpsimd", lambda g: g.memset(ident[:], 0.0))
    p.op("gpsimd", lambda g: g.affine_select(
        out=ident[:], in_=ident[:], compare_op=ALU.not_equal, fill=1.0,
        base=0, pattern=[[-1, 128]], channel_multiplier=1))
    p.op("gpsimd", lambda g: g.memset(zz[:], 0.0))
    for j in range(2):
        p.op("gpsimd", lambda g, j=j: g.affine_select(
            out=maskC[j][:], in_=zz[:], compare_op=ALU.is_ge, fill=-1e30,
            base=-128 * j, pattern=[[1, 256]], channel_multiplier=-1))
    p.dma("sync", lambda e: e.dma_start(out=Bf[:], in_=bf_in))
    p.dma("sync", lambda e: e.dma_start(out=Cf[:], in_=cf_in))
    p.dma("gpsimd", lambda e: e.dma_start(out=Bfb[:], in_=bf_in))
    p.dma("gpsimd", lambda e: e.dma_start(out=Cfb[:], in_=cf_in))
    p.dma("sync", lambda e: e.dma_start(out=dt_r[:], in_=dt_in))
    p.dma("sync", lambda e: e.dma_start(out=acol[:], in_=al_in))
    p.barrier()
    p.op("scalar", lambda s: s.activation(acol[:], acol[:], AF.Exp), inc=ev)
    p.wait("vector", ev)
    p.op("vector", lambda v: v.tensor_scalar(acol[:], acol[:], -1.0, None, ALU.mult),
         inc=ev)
    p.wait("vector", ev)
    p.op("vector", lambda v: v.tensor_scalar(dA[:], dt_r[:], acol[:], None, ALU.mult))
    for c in range(16):
        sl = slice(c * 256, (c + 1) * 256)
        p.op("vector", lambda v, sl=sl: v.tensor_tensor_scan(
            cum[:, sl], dA[:, sl], dA[:, sl], 0.0, ALU.add, ALU.bypass),
            inc=scn)
    p.wait("vector", scn)
    p.op("vector", lambda v: v.tensor_copy(cdr[:], cum[:, 255::256]))
    for c in range(16):
        sl = slice(c * 256, (c + 1) * 256)
        p.op("vector", lambda v, sl=sl, c=c: v.tensor_scalar(
            wrow[:, sl], cum[:, sl], cum[:, c * 256 + 255:c * 256 + 256], None,
            ALU.subtract), inc=(ev if c == 15 else None))
    p.wait("scalar", ev)
    p.op("scalar", lambda s: s.activation(sdor[:], cum[:], AF.Exp))
    p.op("scalar", lambda s: s.activation(wrow[:], wrow[:], AF.Exp, scale=-1.0))
    p.op("scalar", lambda s: s.activation(cdr[:], cdr[:], AF.Exp), inc=ev)
    p.wait("vector", ev)
    p.op("vector", lambda v: v.tensor_tensor(wrow[:], wrow[:], dt_r[:], ALU.mult),
         inc=ev)
    p.barrier()
    p.dma("sync", lambda e: e.dma_start(out=cumdram, in_=cum[:]))
    p.dma("sync", lambda e: e.dma_start(out=sdodram, in_=sdor[:]))
    p.dma("sync", lambda e: e.dma_start(out=cddram, in_=cdr[:]))
    p.barrier()
    p.dma("sync", lambda e: e.dma_start(
        out=cdb[:], in_=cddram.rearrange("a b -> (a b)").rearrange(
            "(x y) -> x y", x=1).broadcast_to([128, NH * 16])))
    p.dma("gpsimd", lambda e: e.dma_start(
        out=dcol[:], in_=dp_in.broadcast_to([128, NH])))
    for (rows, dstT) in ((cum, cumT), (dt_r, dtT), (wrow, wT)):
        for t in range(32):
            p.op("tensor", lambda tt, rows=rows, t=t: tt.transpose(
                ps_tr[:], rows[:, t * 128:(t + 1) * 128], ident[0:NH, 0:NH]),
                inc=mmd)
            p.wait("vector", mmd)
            p.op("vector", lambda v, dstT=dstT, t=t: v.tensor_copy(
                dstT[:, t, :], ps_tr[:]), inc=ev)
            p.wait("tensor", ev)
    p.op("gpsimd", lambda g: g.memset(Ssb[:], 0.0))
    p.barrier()

    for c in range(16):
        csl = slice(c * 256, (c + 1) * 256)
        for jt in range(2):
            p.op("tensor", lambda t, c=c, jt=jt, csl=csl: t.matmul(
                ps_cbt[:], Bfb[:, c * 256 + jt * 128:c * 256 + (jt + 1) * 128],
                Cfb[:, csl], start=True, stop=True), inc=mmd)
            p.wait("vector", mmd)
            p.op("vector", lambda v, jt=jt: v.tensor_copy(cbt[:, jt, :], ps_cbt[:]),
                 inc=ev)
            p.wait("tensor", ev)
        p.dma("gpsimd", lambda e, c=c: e.dma_start(
            out=bt2[:], in_=bt_in.rearrange("(cc lt pp) n -> pp (cc lt) n", pp=128, lt=2)
            [:, 2 * c:2 * c + 2, :]))
        p.barrier()
        for h in range(NH):
            par = c % 2
            p.dma("gpsimd", lambda e, c=c, h=h: e.dma_start(
                out=xb2[:], in_=x_in.rearrange("(cc lt pp) d -> pp (cc lt) d",
                                               pp=128, lt=2)
                [:, 2 * c:2 * c + 2, h * 64:(h + 1) * 64]))
            p.dma("sync", lambda e, c=c, h=h: e.dma_start(
                out=m1[:], in_=cumdram[h:h + 1, c * 256:(c + 1) * 256]
                .broadcast_to([128, 256])))
            p.dma("scalar", lambda e, c=c, h=h: e.dma_start(
                out=sdob[:], in_=sdodram[h:h + 1, c * 256:(c + 1) * 256]
                .broadcast_to([128, 256])))
            p.barrier()
            p.op("vector", lambda v, h=h, par=par: v.tensor_copy(
                Sbf[:], Ssb[:, h, par, :]))
            p.op("vector", lambda v, csl=csl: v.tensor_tensor(
                cwb[:], Cf[:, csl], sdob[:], ALU.mult))
            for jt in range(2):
                tile = 2 * c + jt
                p.op("vector", lambda v, jt=jt, tile=tile, h=h: v.tensor_scalar(
                    diff[:], m1[:], cumT[:, tile, h:h + 1], None, ALU.subtract))
                p.op("vector", lambda v, jt=jt: v.tensor_tensor(
                    diff[:], diff[:], maskC[jt][:], ALU.min), inc=ev)
                p.wait("scalar", ev)
                p.op("scalar", lambda s, jt=jt: s.activation(pte[:], diff[:], AF.Exp),
                     inc=ev)
                p.wait("vector", ev)
                p.op("vector", lambda v, jt=jt, tile=tile, h=h: v.tensor_scalar(
                    pte[:], pte[:], dtT[:, tile, h:h + 1], None, ALU.mult))
                p.op("vector", lambda v, jt=jt: v.tensor_tensor(
                    stb[jt][:], pte[:], cbt[:, jt, :], ALU.mult))
                p.op("vector", lambda v, jt=jt, tile=tile, h=h: v.tensor_scalar(
                    bwb[:, jt, :], bt2[:, jt, :], wT[:, tile, h:h + 1], None,
                    ALU.mult))
            p.barrier()
            for it in range(2):
                for jt in range(2):
                    p.op("tensor", lambda t, it=it, jt=jt: t.matmul(
                        ps_y[it][:], stb[jt][:, it * 128:(it + 1) * 128],
                        xb2[:, jt, :], start=(jt == 0), stop=False))
                p.op("tensor", lambda t, it=it: t.matmul(
                    ps_y[it][:], cwb[:, it * 128:(it + 1) * 128], Sbf[:],
                    start=False, stop=True))
            for jt in range(2):
                p.op("tensor", lambda t, jt=jt: t.matmul(
                    ps_st[:], bwb[:, jt, :], xb2[:, jt, :], start=(jt == 0),
                    stop=(jt == 1)))
            p.barrier()
            for it in range(2):
                p.op("vector", lambda v, it=it, h=h: v.scalar_tensor_tensor(
                    yhb[:], xb2[:, it, :], dcol[:, h:h + 1], ps_y[it][:],
                    ALU.mult, ALU.add), inc=evd)
                p.wait("sync", evd)
                p.dma("sync", lambda e, c=c, it=it, h=h: e.dma_start(
                    out=y_out[c * 256 + it * 128:c * 256 + (it + 1) * 128,
                              h * 64:(h + 1) * 64], in_=yhb[:]))
                p.barrier()
            p.op("vector", lambda v, h=h, par=par, c=c: v.scalar_tensor_tensor(
                Ssb[:, h, 1 - par, :], Ssb[:, h, par, :],
                cdb[:, h * 16 + c:h * 16 + c + 1], ps_st[:],
                ALU.mult, ALU.add))
            p.barrier()

    in_maps = []
    for c in range(NCORES):
        b, hg = c // 2, c % 2
        cols = slice(b * SEQ, (b + 1) * SEQ)
        in_maps.append({
            "x": np.ascontiguousarray(x_tm_host[cols, hg * 2048:(hg + 1) * 2048]),
            "bf": np.ascontiguousarray(B_fm_host[:, cols]),
            "cf": np.ascontiguousarray(C_fm_host[:, cols]),
            "bt": np.ascontiguousarray(B_tm_host[cols, :]),
            "dt": np.ascontiguousarray(dt_host[hg * NH:(hg + 1) * NH, cols]),
            "al": np.ascontiguousarray(A_log[hg * NH:(hg + 1) * NH].reshape(NH, 1).astype(np.float32)),
            "dp": np.ascontiguousarray(D_param[hg * NH:(hg + 1) * NH].reshape(1, NH).astype(np.float32))})
    res = p.run(in_maps)
    yh = np.zeros((TOK, 4096), dtype=bfdt)
    for c in range(NCORES):
        b, hg = c // 2, c % 2
        yh[b * SEQ:(b + 1) * SEQ, hg * 2048:(hg + 1) * 2048] = res[c]["y"]
    return yh



def launch_mamba_gate(yh_fm, gate_fm, norm_w):
    p = Prog("mgate")
    DI = 4096
    y_in = p.dram_in("y", [DI, TC], BF16)
    g_in = p.dram_in("g", [DI, TC], BF16)
    nw_in = p.dram_in("nw", [128, 32], F32)
    h_out = p.dram_out("h", [DI, TC], BF16)
    hid_dram = p.dram_tmp("hid", [DI, TC], F32)

    yb = p.sbuf("yb", [128, TC], BF16)
    gb = p.sbuf("gb", [128, TC], BF16)
    sg = p.sbuf("sg", [128, TC], F32)
    hid = p.sbuf("hid", [128, TC], F32)
    nw = p.sbuf("nw", [128, 32], F32)
    hb = p.sbuf("hbo", [128, TC], BF16)
    tmp = p.sbuf("tmp", [128, TC], F32)
    acc = LNAcc(p, TC, tag="rmsg", want_s1=False)
    ev = p.sem("ev")
    p.dma("scalar", lambda e: e.dma_start(out=nw[:], in_=nw_in))
    p.barrier()
    for ci in range(32):
        r = slice(ci * 128, (ci + 1) * 128)
        p.dma("sync", lambda e, r=r: e.dma_start(out=yb[:], in_=y_in[r, :]))
        p.dma("scalar", lambda e, r=r: e.dma_start(out=gb[:], in_=g_in[r, :]))
        p.barrier()
        p.op("scalar", lambda s: s.activation(sg[:], gb[:], AF.Silu), inc=ev)
        p.wait("vector", ev)
        p.op("vector", lambda v: v.tensor_tensor(hid[:], yb[:], sg[:], ALU.mult),
             inc=ev)
        p.wait("scalar", ev)
        p.wait("tensor", ev)
        for nb in range(4):
            acc.slice_stats(hid[:, nb * 512:(nb + 1) * 512], nb,
                            first=(ci == 0), last=(ci == 31))
        p.wait("sync", ev)
        p.dma("sync", lambda e, ci=ci: e.dma_start(
            out=hid_dram[ci * 128:(ci + 1) * 128, :], in_=hid[:]))
        p.barrier()
    acc.rms_finalize(DI, 1e-5)
    for ci in range(32):
        p.dma("sync", lambda e, ci=ci: e.dma_start(
            out=hid[:], in_=hid_dram[ci * 128:(ci + 1) * 128, :]))
        p.barrier()
        p.op("vector", lambda v: v.tensor_tensor(tmp[:], hid[:], acc.A[:], ALU.mult))
        p.op("vector", lambda v, ci=ci: v.tensor_scalar(
            hb[:], tmp[:], nw[:, ci:ci + 1], None, ALU.mult), inc=ev)
        p.wait("sync", ev)
        p.dma("sync", lambda e, ci=ci: e.dma_start(
            out=h_out[ci * 128:(ci + 1) * 128, :], in_=hb[:]))
        p.barrier()
    in_maps = [{"y": np.ascontiguousarray(yh_fm[:, c * TC:(c + 1) * TC]),
                "g": np.ascontiguousarray(gate_fm[:, c * TC:(c + 1) * TC]),
                "nw": np.ascontiguousarray(norm_w.astype(np.float32).reshape(32, 128).T)}
               for c in range(NCORES)]
    res = p.run(in_maps)
    return np.concatenate([r["h"] for r in res], axis=1)



def kernel(L0_attn_a_x, L0_attn_a_w_qkv, L0_attn_a_w_proj, L0_attn_a_b_proj,
           L0_attn_b_Wq, L0_attn_b_Wk, L0_attn_b_Wv, L0_attn_b_Wo,
           L0_ffn_lut, L0_ffn_gate_idx, L0_ffn_up_idx, L0_ffn_down_idx,
           L1_attn_a_in_proj_w, L1_attn_a_conv_w, L1_attn_a_conv_b,
           L1_attn_a_dt_bias, L1_attn_a_A_log, L1_attn_a_D_param,
           L1_attn_a_norm_w, L1_attn_a_out_proj_w,
           L1_attn_b_w_q, L1_attn_b_w_k, L1_attn_b_w_v, L1_attn_b_w_o,
           L1_ffn_gate_up_packed, L1_ffn_gate_up_scales,
           L1_ffn_down_packed, L1_ffn_down_scales):
    f32 = np.float32
    x0_fm = np.ascontiguousarray(
        np.asarray(L0_attn_a_x, f32).reshape(TOK, D).T)

    wg, wu, wd = launch_ghost_dequant(
        np.ascontiguousarray(np.asarray(L0_ffn_gate_idx).T),
        np.ascontiguousarray(np.asarray(L0_ffn_up_idx).T),
        np.ascontiguousarray(np.asarray(L0_ffn_down_idx).T),
        np.asarray(L0_ffn_lut, f32))
    w_gu_nat, w_dn_nat = launch_fp4_dequant(
        np.ascontiguousarray(np.asarray(L1_ffn_gate_up_packed).view(np.uint8)),
        np.ascontiguousarray(np.asarray(L1_ffn_gate_up_scales).T),
        np.ascontiguousarray(np.asarray(L1_ffn_down_packed).view(np.uint8)),
        np.ascontiguousarray(np.asarray(L1_ffn_down_scales).T))
    wguT = np.ascontiguousarray(w_gu_nat.T)
    wg1 = np.ascontiguousarray(wguT[:, :5632])
    wu1 = np.ascontiguousarray(wguT[:, 5632:])
    wd1 = np.ascontiguousarray(w_dn_nat.T)

    def T(a):
        return np.ascontiguousarray(np.asarray(a, f32).T)

    xn = launch_ln(x0_fm)
    o = launch_lin_attn(xn, np.asarray(L0_attn_a_w_qkv, f32))
    x, xn = launch_proj_res_ln(o, T(L0_attn_a_w_proj),
                               np.asarray(L0_attn_a_b_proj, f32), x0_fm)
    h = launch_ffn_up(xn, wg, wu)
    x, xn = launch_proj_res_ln(h, wd, None, x)
    q, k, v = launch_qkv(xn, T(L0_attn_b_Wq), T(L0_attn_b_Wk), T(L0_attn_b_Wv))
    o = launch_attention(q, k, v)
    x, xn = launch_proj_res_ln(o, T(L0_attn_b_Wo), None, x)
    h = launch_ffn_up(xn, wg, wu)
    x, xn = launch_proj_res_ln(h, wd, None, x)
    gate, hbc, dt = launch_mamba_proj(xn, np.asarray(L1_attn_a_in_proj_w, f32),
                                      np.asarray(L1_attn_a_dt_bias, f32))
    xbc = launch_conv(hbc, np.asarray(L1_attn_a_conv_w, f32),
                      np.asarray(L1_attn_a_conv_b, f32))
    x_tm = np.ascontiguousarray(xbc[:4096].T)
    B_fm = np.ascontiguousarray(xbc[4096:4224])
    C_fm = np.ascontiguousarray(xbc[4224:4352])
    B_tm = np.ascontiguousarray(B_fm.T)
    yh_tm = launch_ssm(x_tm, B_fm, C_fm, B_tm, dt,
                       np.asarray(L1_attn_a_A_log, f32),
                       np.asarray(L1_attn_a_D_param, f32))
    yh_fm = np.ascontiguousarray(yh_tm.T)
    hn = launch_mamba_gate(yh_fm, gate, np.asarray(L1_attn_a_norm_w, f32))
    x, xn = launch_proj_res_ln(hn, np.asarray(L1_attn_a_out_proj_w, f32), None, x)
    h = launch_ffn_up(xn, wg1, wu1)
    x, xn = launch_proj_res_ln(h, wd1, None, x, rms_pre_eps=1e-6)
    q, k, v = launch_qkv(xn, T(L1_attn_b_w_q), T(L1_attn_b_w_k), T(L1_attn_b_w_v))
    o = launch_attention(q, k, v)
    x, xn = launch_proj_res_ln(o, T(L1_attn_b_w_o), None, x)
    h = launch_ffn_up(xn, wg1, wu1)
    x, _ = launch_proj_res_ln(h, wd1, None, x, rms_pre_eps=1e-6, skip_ln=True)
    return np.ascontiguousarray(x.T).reshape(4, SEQ, D).astype(np.float32)
